# version 17
# baseline (speedup 1.0000x reference)
import numpy as np
import ml_dtypes
from contextlib import ExitStack

import concourse.bass as bass
import concourse.mybir as mybir
from concourse.bass_utils import run_bass_kernel_spmd

F32 = mybir.dt.float32
BF16 = mybir.dt.bfloat16
AF = mybir.ActivationFunctionType
ALU = mybir.AluOpType
AX = mybir.AxisListType

NCORES = 8
SEQ = 2048
DM = 1024
H = 16
W = 128
FI = 2816
CHUNK = 512
TPC = CHUNK // 128
WPC = CHUNK // W

WGQ = 0
WUQ = 22528
WDQ = 45056
A_WQ = 0
A_WK = 8192
A_WV = 16384
A_WO = 24576
A_XNT = 32768
A_QT = 36864
A_KT = 40960
A_VT = 46080
A_PT = 51280
A_XN1 = 52304
A_JNK = 54352
F_G = 67584
F_SGN = 70400
F_SG = 73216
F_H = 76032
F_XQT = 78848
F_HQT = 82944
F_XQ = 82944
G_GP = 68608
T_GP = 82944
NBF = 84992

X_SB = 0
OUTB = 8192
AM = 9216
SC = 9392
RC = 9568
LNS = 9744
RS = 9752
NF32 = 9768

PS_AVS = 0
PS_SIMA = 2048
PS_SIMB = 2560
PS_MM0 = 3072
PS_MM1 = 3584
PS_WO = 3072
PS_DN = 0
PS_QU = 3072

ENGS = ("sync", "scalar", "vector", "tensor", "gpsimd")


class Sem:
    def __init__(self, nc, es, name):
        self.h = es.enter_context(nc.semaphore(name))
        self.n = 0

    def take(self, k=1):
        self.n += k
        return self.n


class Prog:
    def __init__(self, nc):
        self.nc = nc
        self.streams = {e: [] for e in ENGS}

    def on(self, eng, fn):
        self.streams[eng].append(fn)

    def emit(self):
        with self.nc.Block() as b:
            for eng in ENGS:
                fns = self.streams[eng]
                getattr(b, eng)(lambda e, fns=fns: [f(e) for f in fns])


def build_program(n_seq):
    n_tok = n_seq * SEQ
    n_chunks = n_tok // CHUNK
    cps = SEQ // CHUNK

    nc = bass.Bass("TRN2", target_bir_lowering=False)
    x_d = nc.dram_tensor("x", [n_tok, DM], F32, kind="ExternalInput").ap()
    wq_d = nc.dram_tensor("wq", [DM, DM], F32, kind="ExternalInput").ap()
    wkv_d = nc.dram_tensor("wkv", [DM, 2 * DM], F32, kind="ExternalInput").ap()
    wo_d = nc.dram_tensor("wo", [DM, DM], F32, kind="ExternalInput").ap()
    bo_d = nc.dram_tensor("bo", [1, DM], F32, kind="ExternalInput").ap()
    wg_d = nc.dram_tensor("wgate", [FI, DM], F32, kind="ExternalInput").ap()
    wu_d = nc.dram_tensor("wup", [FI, DM], F32, kind="ExternalInput").ap()
    wd_d = nc.dram_tensor("wdown", [DM, FI], F32, kind="ExternalInput").ap()
    mask_d = nc.dram_tensor("mask01", [W, W], BF16, kind="ExternalInput").ap()
    ident_d = nc.dram_tensor("ident", [W, W], BF16, kind="ExternalInput").ap()
    out_d = nc.dram_tensor("out", [n_tok, DM], F32, kind="ExternalOutput").ap()
    xn1s = nc.dram_tensor("xn1s", [CHUNK, DM], BF16).ap()
    aos = nc.dram_tensor("aos", [CHUNK, DM], BF16).ap()
    x1s = nc.dram_tensor("x1s", [n_tok, DM], F32).ap()
    xqs = nc.dram_tensor("xqs", [n_tok, DM], BF16).ap()
    hqs = nc.dram_tensor("hqs", [CHUNK, FI], BF16).ap()
    wgqs = nc.dram_tensor("wgqs", [FI, DM], BF16).ap()
    wuqs = nc.dram_tensor("wuqs", [FI, DM], BF16).ap()
    wdqs = nc.dram_tensor("wdqs", [DM, FI], BF16).ap()

    with ExitStack() as es:
        bfar = es.enter_context(nc.sbuf_tensor("bfar", [128, NBF], BF16))
        far = es.enter_context(nc.sbuf_tensor("far", [128, NF32], F32))
        mask_sb = es.enter_context(nc.sbuf_tensor("mask_sb", [W, W], BF16))
        ident_sb = es.enter_context(nc.sbuf_tensor("ident_sb", [W, W], BF16))
        ones_c = es.enter_context(nc.sbuf_tensor("ones_c", [128, 1], BF16))
        ones_r = es.enter_context(nc.sbuf_tensor("ones_r", [1, 128], BF16))
        bo_sb = es.enter_context(nc.sbuf_tensor("bo_sb", [1, DM], BF16))
        cneg3 = es.enter_context(nc.sbuf_tensor("cneg3", [128, 1], F32))
        ceighth = es.enter_context(nc.sbuf_tensor("ceighth", [128, 1], F32))
        ps = es.enter_context(nc.psum_tensor("ps", [128, 4096], F32))

        sL = Sem(nc, es, "sL")
        sL2 = Sem(nc, es, "sL2")
        sT = Sem(nc, es, "sT")
        sT2 = Sem(nc, es, "sT2")
        sG = Sem(nc, es, "sG")
        sV = Sem(nc, es, "sV")
        sA = Sem(nc, es, "sA")
        sP = Sem(nc, es, "sP")
        sGC = Sem(nc, es, "sGC")
        bar = Sem(nc, es, "bar")

        def bfv(off, dims):
            return bass.AP(bfar, off, [[NBF, 128]] + dims)

        def fv(off, dims):
            return bass.AP(far, off, [[NF32, 128]] + dims)

        p = Prog(nc)

        def barrier():
            tgt = bar.take(5)
            waits = [(sL.h, sL.n), (sL2.h, sL2.n), (sT.h, sT.n),
                     (sT2.h, sT2.n), (sG.h, sG.n), (sV.h, sV.n),
                     (sA.h, sA.n), (sP.h, sP.n), (sGC.h, sGC.n)]
            for eng in ENGS:
                def bfn(e, waits=waits, tgt=tgt):
                    for h, n in waits:
                        e.wait_ge(h, n)
                    e.nop().then_inc(bar.h, 1)
                    e.wait_ge(bar.h, tgt)
                p.on(eng, bfn)

        def emit_ln(x_ap, out_ap, junk_ap):
            ssq = fv(LNS + 0, [[1, 1]])
            sm = fv(LNS + 1, [[1, 1]])
            mean = fv(LNS + 2, [[1, 1]])
            veps = fv(LNS + 3, [[1, 1]])
            rs0 = fv(LNS + 4, [[1, 1]])
            t1 = fv(LNS + 5, [[1, 1]])
            rstd = fv(LNS + 6, [[1, 1]])
            bias = fv(LNS + 7, [[1, 1]])
            vprev = sV.n
            aw = sA.take(1)

            def sc1(e, x_ap=x_ap, junk_ap=junk_ap, vprev=vprev):
                e.wait_ge(sV.h, vprev)
                e.activation(out=junk_ap, in_=x_ap, func=AF.Square,
                             accum_out=ssq)
                e.drain()
                e.activation(out=junk_ap, in_=x_ap, func=AF.Copy,
                             accum_out=sm).then_inc(sA.h, 1)
            p.on("scalar", sc1)
            vw = sV.take(1)

            def v1(e, x_ap=x_ap, aw=aw):
                e.wait_ge(sA.h, aw)
                e.drain()
                e.tensor_scalar(out=mean, in0=sm, scalar1=1.0 / DM,
                                scalar2=None, op0=ALU.mult)
                e.drain()
                e.tensor_tensor(out=t1, in0=mean, in1=mean, op=ALU.mult)
                e.tensor_scalar(out=veps, in0=ssq, scalar1=1.0 / DM,
                                scalar2=None, op0=ALU.mult)
                e.drain()
                e.tensor_tensor(out=veps, in0=veps, in1=t1, op=ALU.subtract)
                e.drain()
                e.tensor_scalar(out=veps, in0=veps, scalar1=1e-5,
                                scalar2=None, op0=ALU.add).then_inc(sV.h, 1)
            p.on("vector", v1)
            aw2 = sA.take(1)

            def sc2(e, vw=vw):
                e.wait_ge(sV.h, vw)
                e.activation(out=rs0, in_=veps, func=AF.Sqrt).then_inc(sA.h, 1)
            p.on("scalar", sc2)
            sV.take(1)

            def v2(e, x_ap=x_ap, out_ap=out_ap, aw2=aw2):
                e.wait_ge(sA.h, aw2)
                e.reciprocal(out=rs0, in_=rs0)
                e.drain()
                e.tensor_tensor(out=t1, in0=rs0, in1=rs0, op=ALU.mult)
                e.drain()
                e.tensor_tensor(out=t1, in0=t1, in1=veps, op=ALU.mult)
                e.drain()
                e.tensor_scalar(out=t1, in0=t1, scalar1=-0.5, scalar2=1.5,
                                op0=ALU.mult, op1=ALU.add)
                e.drain()
                e.tensor_tensor(out=rstd, in0=rs0, in1=t1, op=ALU.mult)
                e.drain()
                e.scalar_tensor_tensor(out=bias, in0=mean, scalar=-1.0,
                                       in1=rstd, op0=ALU.mult, op1=ALU.mult)
                e.drain()
                e.tensor_scalar(out=out_ap, in0=x_ap, scalar1=rstd,
                                scalar2=bias, op0=ALU.mult,
                                op1=ALU.add).then_inc(sV.h, 1)
            p.on("vector", v2)

        def emit_quant(xf2, xf3, q3, F, uoff=0, nboff=0, gguard=None,
                       usb=None):
            nb = F // 16
            am2 = fv(AM + nboff, [[1, nb]])
            am3 = fv(AM + nboff, [[1, nb], [1, 1]])
            sc = fv(SC + nboff, [[1, nb]])
            sc_b = fv(SC + nboff, [[1, nb], [0, 16]])
            rc = fv(RC + nboff, [[1, nb]])
            rc_b = fv(RC + nboff, [[1, nb], [0, 16]])
            if usb is None:
                u = bass.AP(ps, uoff, [[4096, 128], [1, F]])
                u3 = bass.AP(ps, uoff, [[4096, 128], [16, nb], [1, 16]])
            else:
                u = fv(usb, [[1, F]])
                u3 = fv(usb, [[16, nb], [1, 16]])
            g = bfv(F_G, [[1, F]])
            g3 = bfv(F_G, [[16, nb], [1, 16]])
            sgn = bfv(F_SGN, [[1, F]])
            vprev = sV.n
            aw = sA.take(1)

            def sc1(e, xf2=xf2, vprev=vprev):
                e.wait_ge(sV.h, vprev)
                e.activation(out=sgn, in_=xf2, func=AF.Sign).then_inc(sA.h, 1)
            p.on("scalar", sc1)
            sV.take(1)

            def v1(e, xf3=xf3, q3=q3, aw=aw, u=u, u3=u3, g=g, g3=g3, sgn=sgn, gguard=gguard,
                   am2=am2, am3=am3, sc=sc, sc_b=sc_b, rc=rc, rc_b=rc_b):
                e.drain()
                e.tensor_reduce(out=am3, in_=xf3, axis=AX.X, op=ALU.max,
                                apply_absolute_value=True)
                e.drain()
                e.tensor_scalar(out=sc, in0=am2, scalar1=1.0 / 6.0,
                                scalar2=1e-12, op0=ALU.mult, op1=ALU.max)
                e.drain()
                e.reciprocal(out=rc, in_=sc)
                e.drain()
                e.scalar_tensor_tensor(out=u, in0=xf2, scalar=-1.0, in1=xf2,
                                       op0=ALU.mult, op1=ALU.max)
                e.drain()
                e.tensor_tensor(out=u3, in0=u3, in1=rc_b, op=ALU.mult)
                e.drain()
                e.tensor_scalar(out=g, in0=u, scalar1=0.25, scalar2=None,
                                op0=ALU.is_gt)
                for t in (0.75, 1.25, 1.75):
                    e.drain()
                    e.scalar_tensor_tensor(out=g, in0=u, scalar=t, in1=g,
                                           op0=ALU.is_gt, op1=ALU.add)
                e.drain()
                e.tensor_scalar(out=g, in0=g, scalar1=0.5, scalar2=None,
                                op0=ALU.mult)
                for t in (2.5, 3.5, 5.0, 5.0):
                    e.drain()
                    e.scalar_tensor_tensor(out=g, in0=u, scalar=t,
                                           in1=g, op0=ALU.is_gt,
                                           op1=ALU.add)
                e.wait_ge(sA.h, aw)
                e.drain()
                e.tensor_tensor(out=g, in0=g, in1=sgn, op=ALU.mult)
                if gguard is not None:
                    e.wait_ge(sG.h, gguard)
                e.drain()
                e.tensor_tensor(out=q3, in0=g3, in1=sc_b,
                                op=ALU.mult).then_inc(sV.h, 1)
            p.on("vector", v1)
            return sV.n

        def emit_quant_gp(xf2, xf3, q3, F, nboff, usb):
            nb = F // 16
            am2 = fv(AM + nboff, [[1, nb]])
            am3 = fv(AM + nboff, [[1, nb], [1, 1]])
            sc = fv(SC + nboff, [[1, nb]])
            sc_b = fv(SC + nboff, [[1, nb], [0, 16]])
            rc = fv(RC + nboff, [[1, nb]])
            rc_b = fv(RC + nboff, [[1, nb], [0, 16]])
            u = fv(usb, [[1, F]])
            u3 = fv(usb, [[16, nb], [1, 16]])
            g = bfv(G_GP, [[1, F]])
            g3 = bfv(G_GP, [[16, nb], [1, 16]])
            tt = bfv(T_GP, [[1, F]])
            sgn = bfv(F_SGN + 1024, [[1, F]])
            vprev = sV.n
            aw = sA.take(1)

            def sc1(e, xf2=xf2, u=u, vprev=vprev):
                e.wait_ge(sV.h, vprev)
                e.activation(out=sgn, in_=xf2, func=AF.Sign)
                e.drain()
                e.activation(out=u, in_=xf2, func=AF.Abs).then_inc(sA.h, 1)
            p.on("scalar", sc1)
            vq = sV.take(1)

            def v1(e, xf3=xf3, am2=am2, am3=am3, sc=sc, rc=rc):
                e.drain()
                e.tensor_reduce(out=am3, in_=xf3, axis=AX.X, op=ALU.max,
                                apply_absolute_value=True)
                e.drain()
                e.tensor_scalar(out=sc, in0=am2, scalar1=1.0 / 6.0,
                                scalar2=1e-12, op0=ALU.mult, op1=ALU.max)
                e.drain()
                e.reciprocal(out=rc, in_=sc).then_inc(sV.h, 1)
            p.on("vector", v1)
            gq = sGC.take(1)

            def gp1(e, xf2=xf2, xf3=xf3, q3=q3, u=u, u3=u3, g=g, g3=g3,
                    tt=tt, sgn=sgn, sc_b=sc_b, rc_b=rc_b, vq=vq, aw=aw):
                e.wait_ge(sV.h, vq)
                e.wait_ge(sA.h, aw)
                e.tensor_tensor(out=u3, in0=u3, in1=rc_b, op=ALU.mult)
                e.drain()
                e.tensor_scalar(out=g, in0=u, scalar1=0.25, scalar2=0.5,
                                op0=ALU.is_gt, op1=ALU.mult)
                for t, m in ((0.75, 0.5), (1.25, 0.5), (1.75, 0.5),
                             (2.5, 1.0), (3.5, 1.0), (5.0, 2.0)):
                    e.drain()
                    e.tensor_scalar(out=tt, in0=u, scalar1=t, scalar2=m,
                                    op0=ALU.is_gt, op1=ALU.mult)
                    e.drain()
                    e.tensor_tensor(out=g, in0=g, in1=tt, op=ALU.add)
                e.drain()
                e.tensor_tensor(out=g, in0=g, in1=sgn, op=ALU.mult)
                e.drain()
                e.tensor_tensor(out=q3, in0=g3, in1=sc_b,
                                op=ALU.mult).then_inc(sGC.h, 1)
            p.on("gpsimd", gp1)
            return gq

        sG.take(80)

        def p0_gp(e):
            e.dma_start(
                out=bfv(A_WQ, [[DM, 8], [1, DM]]),
                in_=bass.AP(wq_d.tensor, 0,
                            [[DM, 128], [DM * 128, 8], [1, DM]])
            ).then_inc(sG.h, 16)
            e.dma_start(
                out=bfv(A_WK, [[DM, 8], [1, DM]]),
                in_=bass.AP(wkv_d.tensor, 0,
                            [[2 * DM, 128], [2 * DM * 128, 8], [1, DM]])
            ).then_inc(sG.h, 16)
            e.dma_start(
                out=bfv(A_WV, [[DM, 8], [1, DM]]),
                in_=bass.AP(wkv_d.tensor, DM,
                            [[2 * DM, 128], [2 * DM * 128, 8], [1, DM]])
            ).then_inc(sG.h, 16)
            e.dma_start(
                out=bfv(A_WO, [[DM, 8], [1, DM]]),
                in_=bass.AP(wo_d.tensor, 0,
                            [[DM, 128], [DM * 128, 8], [1, DM]])
            ).then_inc(sG.h, 16)
            e.dma_start(out=bo_sb[:], in_=bo_d).then_inc(sG.h, 16)
        p.on("gpsimd", p0_gp)
        sL.take(32)
        p.on("sync", lambda e: (e.dma_start(out=mask_sb[:],
                                            in_=mask_d).then_inc(sL.h, 16),
                                e.dma_start(out=ident_sb[:],
                                            in_=ident_d).then_inc(sL.h, 16)))
        p.on("vector", lambda e: (e.memset(ones_c[:], 1.0),
                                  e.memset(ones_r[:], 1.0),
                                  e.memset(cneg3[:], -3.0),
                                  e.memset(ceighth[:], 0.125),
                                  e.memset(bfv(A_VT + 64,
                                               [[1040, 5], [65, 16], [1, 1]]),
                                           1.0)))

        wjobs = ([(wg_d, wgqs, i, DM) for i in range(FI // 128)]
                 + [(wu_d, wuqs, i, DM) for i in range(FI // 128)]
                 + [(wd_d, wdqs, i, FI) for i in range(DM // 128)])
        jbuf = (X_SB, X_SB + 4096)
        jdone = []
        jstore = []
        for jidx, (src_d, dst_s, i, F) in enumerate(wjobs):
            nb = F // 16
            boff = jbuf[jidx % 2]
            sLx = sL if jidx % 2 == 0 else sL2
            guards = jdone[jidx - 2] if jidx >= 2 else (0, 0)
            lw = sLx.take(16)

            def ld(e, src_d=src_d, i=i, F=F, boff=boff, guards=guards,
                   sLx=sLx):
                e.wait_ge(sV.h, guards[0])
                e.wait_ge(sA.h, guards[1])
                e.dma_start(
                    out=fv(boff, [[1, F]]),
                    in_=bass.AP(src_d.tensor, i * 128 * F,
                                [[F, 128], [1, F]])).then_inc(sLx.h, 16)
            p.on("sync", ld)
            p.on("vector", lambda e, lw=lw, sLx=sLx: e.wait_ge(sLx.h, lw))
            p.on("scalar", lambda e, lw=lw, sLx=sLx: e.wait_ge(sLx.h, lw))
            qdone = emit_quant(fv(boff, [[1, F]]),
                               fv(boff, [[16, nb], [1, 16]]),
                               bfv(F_H, [[16, nb], [1, 16]]), F,
                               gguard=(jstore[-1] if jstore else None))
            jdone.append((sV.n, sA.n))
            jstore.append(sG.take(16))

            def st(e, dst_s=dst_s, i=i, F=F, vq=qdone):
                e.wait_ge(sV.h, vq)
                e.dma_start(
                    out=bass.AP(dst_s.tensor, i * 128 * F, [[F, 128], [1, F]]),
                    in_=bfv(F_H, [[1, F]])).then_inc(sG.h, 16)
            p.on("gpsimd", st)

        barrier()

        exp_bank_guard = [0, 0]
        pt_slot_guard = [0, 0]
        ce = {"tw": 0, "tw2": 0, "wo": 0, "vres": 0, "s2gq": 0,
              "carry": 0, "pdone": 0}
        ce_hist = []
        lbases = {}

        def emit_s1_load(cc):
            guards = ce_hist[cc - 2] if cc >= 2 else (0, 0, 0)
            lbase = sL.take(TPC * 16)
            lbases[cc] = lbase
            xoff = X_SB + (cc % 2) * 4096

            def s1_load(e, cc=cc, guards=guards, xoff=xoff, lbase=lbase):
                e.wait_ge(sL.h, lbase - TPC * 16)
                e.wait_ge(sV.h, guards[0])
                e.wait_ge(sA.h, guards[1])
                e.wait_ge(sG.h, guards[2])
                for t in range(TPC):
                    e.dma_start(
                        out=fv(xoff + t * DM, [[1, DM]]),
                        in_=x_d[cc * CHUNK + t * 128:
                                cc * CHUNK + (t + 1) * 128, :]
                    ).then_inc(sL.h, 16)
            p.on("sync", s1_load)

        def emit_ln1(cc):
            xoff = X_SB + (cc % 2) * 4096
            lbase = lbases[cc]
            for t in range(TPC):
                gprev = sG.n
                p.on("vector", lambda e, lbase=lbase, gprev=gprev:
                     (e.wait_ge(sL.h, lbase), e.wait_ge(sG.h, gprev)))
                p.on("scalar", lambda e, lbase=lbase: e.wait_ge(sL.h, lbase))
                emit_ln(fv(xoff + t * DM, [[1, DM]]),
                        bfv(A_XN1 + (t % 2) * DM, [[1, DM]]),
                        bfv(A_JNK, [[1, DM]]))
                vq = sV.n
                sG.take(16)

                def st(e, t=t, vq=vq, tg=ce["tw"]):
                    e.wait_ge(sV.h, vq)
                    e.wait_ge(sT.h, tg)
                    e.dma_start(
                        out=xn1s[t * 128:(t + 1) * 128, :],
                        in_=bfv(A_XN1 + (t % 2) * DM, [[1, DM]])
                    ).then_inc(sG.h, 16)
                p.on("gpsimd", st)
            ce["s2gq"] = sG.n

        def emit_s2(cc):
            tpre1 = sT.n
            tw = sT.take(8 * 16)
            ce["tw"] = tw

            def s2(e, gq=ce["s2gq"], wg=ce["wo"], tpre1=tpre1):
                e.wait_ge(sG.h, gq)
                e.wait_ge(sP.h, wg)
                e.wait_ge(sT.h, tpre1)
                for k in range(8):
                    e.dma_start_transpose(
                        out=bfv(A_XNT + k * CHUNK, [[1, CHUNK]]),
                        in_=xn1s[:, k * 128:(k + 1) * 128]).then_inc(sT.h, 16)
            p.on("sync", s2)
            return tw

        def emit_carry(cc):
            new_seq = (cc % cps == 0)
            carry_done = sV.take(1)
            ce["carry"] = carry_done

            def s3_carry(e, new_seq=new_seq, pdone=ce["pdone"]):
                e.wait_ge(sA.h, pdone)
                if new_seq:
                    e.memset(bfv(A_KT, [[640, 8], [1, 128]]), 0.0)
                    e.drain()
                    e.memset(bfv(A_VT, [[65, 16], [1, 64]]), 0.0)
                else:
                    e.tensor_copy(out=bfv(A_KT, [[640, 8], [1, 128]]),
                                  in_=bfv(A_KT + 4 * 128, [[640, 8], [1, 128]]))
                    e.drain()
                    e.tensor_copy(out=bfv(A_VT, [[65, 16], [1, 64]]),
                                  in_=bfv(A_VT + 4 * 1040, [[65, 16], [1, 64]]))
                e.engine_nop().then_inc(sV.h, 1)
            p.on("vector", s3_carry)

        def make_projs(cc, tw):
            jobs = []
            for woff, base, pitch in ((A_WQ, A_QT, 512), (A_WK, A_KT, 640)):
                for m in range(8):
                    dsto = base + m * pitch + (128 if base == A_KT else 0)
                    jobs.append((
                        (lambda k, woff=woff, m=m:
                         bfv(woff + k * DM + m * 128, [[1, 128]])),
                        (lambda k: bfv(A_XNT + k * CHUNK, [[1, CHUNK]])),
                        dsto))
            for t in range(TPC):
                for half in range(2):
                    jobs.append((
                        (lambda k, t=t: bfv(A_XNT + k * CHUNK + t * 128,
                                            [[1, 128]])),
                        (lambda k, half=half: bfv(A_WV + k * DM + half * 512,
                                                  [[1, 512]])),
                        (A_VT + (1 + t) * 1040 + half * 520, "v")))
            state = {"slot": 0, "first": True}
            evac_hist = []

            def emit_group(lo, hi, tw=tw):
                for j in range(lo, hi):
                    lhs_fn, rhs_fn, dst_off = jobs[j]
                    col = PS_MM0 if state["slot"] == 0 else PS_MM1
                    bank_guard = (evac_hist[-2] if len(evac_hist) >= 2
                                  else None)
                    pw = sP.take(1)
                    first = state["first"]
                    state["first"] = False

                    def mm(e, col=col, bank_guard=bank_guard, first=first,
                           lhs_fn=lhs_fn, rhs_fn=rhs_fn, tw=tw,
                           carry_done=ce["carry"], vres_g=ce["vres"]):
                        if first:
                            e.wait_ge(sT.h, tw)
                            e.wait_ge(sV.h, carry_done)
                            e.wait_ge(sV.h, vres_g)
                        if bank_guard is not None:
                            e.wait_ge(sA.h, bank_guard)
                        for k in range(8):
                            last = e.matmul(ps[:, col:col + 512], lhs_fn(k),
                                            rhs_fn(k), start=(k == 0),
                                            stop=(k == 7))
                        last.then_inc(sP.h, 1)
                    p.on("tensor", mm)
                    ev_n = sA.take(1)

                    def ev(e, col=col, dst_off=dst_off, pw=pw):
                        e.wait_ge(sP.h, pw)
                        if isinstance(dst_off, tuple):
                            dst = bfv(dst_off[0], [[65, 8], [1, 64]])
                            src = bass.AP(ps, col,
                                          [[4096, 128], [64, 8], [1, 64]])
                        else:
                            dst = bfv(dst_off, [[1, 512]])
                            src = ps[:, col:col + 512]
                        e.activation(out=dst, in_=src,
                                     func=AF.Copy).then_inc(sA.h, 1)
                    p.on("scalar", ev)
                    evac_hist.append(ev_n)
                    state["slot"] ^= 1
                if hi == len(jobs):
                    ce["pdone"] = sA.n
            return emit_group

        def emit_windows(cc):
            for w in range(WPC):
                norm_prev = sV.n
                sim_incs = {}

                def emit_sim(pp, w=w, norm_prev=norm_prev,
                             proj_done=ce["pdone"]):
                    pw = sP.take(1)
                    sim_incs[pp] = pw
                    guard = exp_bank_guard[pp % 2]

                    def mm_sim(e, p=pp, w=w, guard=guard,
                               norm_prev=norm_prev, proj_done=proj_done):
                        if p == 0:
                            e.wait_ge(sA.h, proj_done)
                        if guard:
                            e.wait_ge(sA.h, guard)
                        simc = PS_SIMA + (p % 2) * 1024
                        for j2 in range(2):
                            sub = j2 * 64
                            for j, sl in enumerate((w, w + 1)):
                                e.matmul(
                                    ps[:, simc + j2 * 512 + j * 128:
                                       simc + j2 * 512 + j * 128 + 128],
                                    bass.AP(bfar, sub * NBF + A_KT + p * 640
                                            + sl * 128,
                                            [[NBF, 64], [1, 128]]),
                                    bass.AP(bfar, sub * NBF + A_QT + p * 512
                                            + w * 128, [[NBF, 64], [1, 128]]),
                                    start=True, stop=(j == 0))
                            last = e.matmul(
                                ps[:, simc + j2 * 512 + 128:
                                   simc + j2 * 512 + 256],
                                ident_sb[:], mask_sb[:],
                                start=False, stop=True)
                        last.then_inc(sP.h, 1)
                    p.on("tensor", mm_sim)

                emit_sim(0)
                emit_sim(1)
                for p0 in range(8):
                    simc = PS_SIMA + (p0 % 2) * 1024
                    pt_off = A_PT + (p0 % 2) * 512
                    aw = sA.take(1)
                    exp_bank_guard[p0 % 2] = aw
                    slot_guard = pt_slot_guard[p0 % 2]

                    def act_exp(e, pt_off=pt_off, simc=simc,
                                pw=sim_incs[p0], slot_guard=slot_guard):
                        e.wait_ge(sP.h, pw)
                        if slot_guard:
                            e.wait_ge(sP.h, slot_guard)
                        e.activation(out=bfv(pt_off, [[1, 512]]),
                                     in_=bass.AP(ps, simc,
                                                 [[4096, 128], [512, 2],
                                                  [1, 256]]),
                                     func=AF.Exp, bias=cneg3[:],
                                     scale=ceighth[:]).then_inc(sA.h, 1)
                    p.on("scalar", act_exp)
                    pw2 = sP.take(1)
                    pt_slot_guard[p0 % 2] = pw2

                    def mm_av(e, pt_off=pt_off, w=w, p=p0, aw=aw,
                              norm_prev=norm_prev):
                        e.wait_ge(sA.h, aw)
                        if p == 0:
                            e.wait_ge(sV.h, norm_prev)
                        for j2 in range(2):
                            h = 2 * p + j2
                            for j, sl in enumerate((w, w + 1)):
                                lhsT = bfv(pt_off + j2 * 256 + j * 128,
                                           [[1, 128]])
                                last = e.matmul(
                                    ps[:, PS_AVS + h * 128:
                                       PS_AVS + h * 128 + 65],
                                    lhsT,
                                    bfv(A_VT + sl * 1040 + h * 65, [[1, 65]]),
                                    start=(j == 0), stop=(j == 1))
                        last.then_inc(sP.h, 1)
                    p.on("tensor", mm_av)
                    if p0 + 2 < 8:
                        emit_sim(p0 + 2)
                pw = sP.n
                gprev = sG.n
                vq = sV.take(1)

                def v_norm(e, w=w, pw=pw, gprev=gprev):
                    e.wait_ge(sP.h, pw)
                    e.wait_ge(sG.h, gprev)
                    e.reciprocal(out=fv(RS, [[1, H]]),
                                 in_=bass.AP(ps, 64, [[4096, 128], [128, 16],
                                                      [1, 1]]))
                    e.drain()
                    e.scalar_tensor_tensor(
                        out=bfv(A_XN1 + (w % 2) * DM, [[1, DM]]),
                        in0=bass.AP(ps, 0, [[4096, 128], [128, 16], [1, 64]]),
                        scalar=1.0,
                        in1=fv(RS, [[1, H], [0, 64]]),
                        op0=ALU.mult, op1=ALU.mult).then_inc(sV.h, 1)
                p.on("vector", v_norm)
                sG.take(16)

                def g_st(e, w=w, vq=vq, tg=ce["tw2"]):
                    e.wait_ge(sV.h, vq)
                    e.wait_ge(sT.h, tg)
                    e.dma_start(
                        out=aos[w * 128:(w + 1) * 128, :],
                        in_=bfv(A_XN1 + (w % 2) * DM, [[1, DM]])
                    ).then_inc(sG.h, 16)
                p.on("gpsimd", g_st)

            gq = sG.n
            tpre2 = sT.n
            tw2 = sT.take(8 * 16)
            ce["tw2"] = tw2

            def s5(e, gq=gq, tpre2=tpre2):
                e.wait_ge(sG.h, gq)
                e.wait_ge(sT.h, tpre2)
                for k in range(8):
                    e.dma_start_transpose(
                        out=bfv(A_XNT + k * CHUNK, [[1, CHUNK]]),
                        in_=aos[:, k * 128:(k + 1) * 128]).then_inc(sT.h, 16)
            p.on("sync", s5)
            return tw2

        def emit_tail(cc, tw2):
            xoff = X_SB + (cc % 2) * 4096
            has_next = (cc + 1 < n_chunks)
            wo_pws = []
            vres_vqs = []
            for t in range(TPC):
                vprev = vres_vqs[t - 1] if t > 0 else 0
                pw = sP.take(1)
                wo_pws.append(pw)

                def mm_wo(e, t=t, vprev=vprev, tw2=tw2):
                    e.wait_ge(sT.h, tw2)
                    if t > 0:
                        e.wait_ge(sV.h, vprev)
                    for k in range(8):
                        for half in range(2):
                            e.matmul(
                                ps[:, PS_WO + half * 512:
                                   PS_WO + half * 512 + 512],
                                bfv(A_XNT + k * CHUNK + t * 128, [[1, 128]]),
                                bfv(A_WO + k * DM + half * 512, [[1, 512]]),
                                start=(k == 0), stop=False)
                    for half in range(2):
                        last = e.matmul(ps[:, PS_WO + half * 512:
                                           PS_WO + half * 512 + 512],
                                        ones_r[:],
                                        bo_sb[:, half * 512:(half + 1) * 512],
                                        start=False, stop=True)
                    last.then_inc(sP.h, 1)
                p.on("tensor", mm_wo)
                vq = sV.take(1)
                vres_vqs.append(vq)

                def v_res(e, t=t, pw=pw, xoff=xoff):
                    e.wait_ge(sP.h, pw)
                    e.tensor_tensor(out=fv(xoff + t * DM, [[1, DM]]),
                                    in0=fv(xoff + t * DM, [[1, DM]]),
                                    in1=ps[:, PS_WO:PS_WO + DM],
                                    op=ALU.add).then_inc(sV.h, 1)
                p.on("vector", v_res)
                sG.take(16)

                def g_st(e, cc=cc, t=t, vq=vq, xoff=xoff):
                    e.wait_ge(sV.h, vq)
                    e.dma_start(
                        out=x1s[cc * CHUNK + t * 128:
                                cc * CHUNK + (t + 1) * 128, :],
                        in_=fv(xoff + t * DM, [[1, DM]])).then_inc(sG.h, 16)
                p.on("gpsimd", g_st)
            ce["wo"] = sP.n
            ce["vres"] = vres_vqs[-1]
            emit_pg = None
            if has_next:
                emit_carry(cc + 1)
                emit_ln1(cc + 1)
            for t in range(TPC):
                emit_ln(fv(xoff + t * DM, [[1, DM]]), fv(OUTB, [[1, DM]]),
                        bfv(A_JNK, [[1, DM]]))
                if t == 0 and has_next:
                    tw_n = emit_s2(cc + 1)
                    emit_pg = make_projs(cc + 1, tw_n)
                qdone = emit_quant(fv(OUTB, [[1, DM]]),
                                   fv(OUTB, [[16, 64], [1, 16]]),
                                   bfv(F_XQ, [[16, 64], [1, 16]]), DM,
                                   uoff=PS_AVS, gguard=sG.n)
                sG.take(16)

                def g_st2(e, cc=cc, t=t, vq2=qdone):
                    e.wait_ge(sV.h, vq2)
                    e.dma_start(
                        out=xqs[cc * CHUNK + t * 128:
                                cc * CHUNK + (t + 1) * 128, :],
                        in_=bfv(F_XQ, [[1, DM]])).then_inc(sG.h, 16)
                p.on("gpsimd", g_st2)
                if emit_pg is not None:
                    emit_pg(t * 6, (t + 1) * 6)
            ce_hist.append((sV.n, sA.n, sG.n))

        emit_s1_load(0)
        for c in range(n_chunks):
            if c + 1 < n_chunks:
                emit_s1_load(c + 1)
            if c == 0:
                emit_carry(0)
                emit_ln1(0)
                tw0 = emit_s2(0)
                pg0 = make_projs(0, tw0)
                pg0(0, 24)
            tw2 = emit_windows(c)
            emit_tail(c, tw2)

        barrier()
        sT.take(38 * 16)

        def p2_w(e):
            for k in range(8):
                e.dma_start_transpose(
                    out=bfv(WGQ + k * FI, [[1, FI]]),
                    in_=wgqs[:, k * 128:(k + 1) * 128]).then_inc(sT.h, 16)
            for k in range(8):
                e.dma_start_transpose(
                    out=bfv(WUQ + k * FI, [[1, FI]]),
                    in_=wuqs[:, k * 128:(k + 1) * 128]).then_inc(sT.h, 16)
            for cc in range(22):
                e.dma_start_transpose(
                    out=bfv(WDQ + cc * DM, [[1, DM]]),
                    in_=wdqs[:, cc * 128:(cc + 1) * 128]).then_inc(sT.h, 16)
        p.on("scalar", p2_w)
        barrier()

        nsl = [(i * 512, min(512, FI - i * 512))
               for i in range((FI + 511) // 512)]
        vres2_last = [sV.n]
        up_last = [0]
        lhqt_last = [(sT.n, sT2.n)]
        for c in range(n_chunks):
            lbase = sL.take(TPC * 16)

            def l_x1(e, c=c, vguard=vres2_last[0]):
                e.wait_ge(sV.h, vguard)
                for t in range(TPC):
                    e.dma_start(
                        out=fv(X_SB + t * DM, [[1, DM]]),
                        in_=x1s[c * CHUNK + t * 128:
                                c * CHUNK + (t + 1) * 128, :]
                    ).then_inc(sL.h, 16)
            p.on("sync", l_x1)

            tpre = sT.n
            tw = sT.take(8 * 16)

            def l_xqt(e, c=c, pguard=up_last[0], tpre=tpre):
                e.wait_ge(sP.h, pguard)
                e.wait_ge(sT.h, tpre)
                for k in range(8):
                    e.dma_start_transpose(
                        out=bfv(F_XQT + k * CHUNK, [[1, CHUNK]]),
                        in_=xqs[c * CHUNK:(c + 1) * CHUNK,
                                k * 128:(k + 1) * 128]).then_inc(sT.h, 16)
            p.on("scalar", l_xqt)

            vh_inc = [vres2_last[0]]
            for t in range(TPC):
                vprev = vh_inc[0]
                pw = sP.take(1)

                def mm_gate(e, t=t, vprev=vprev, tw=tw):
                    e.wait_ge(sT.h, tw)
                    e.wait_ge(sV.h, vprev)
                    for k in range(8):
                        for ns, nw in nsl:
                            last = e.matmul(
                                ps[:, ns:ns + nw],
                                bfv(F_XQT + k * CHUNK + t * 128, [[1, 128]]),
                                bfv(WGQ + k * FI + ns, [[1, nw]]),
                                start=(k == 0), stop=(k == 7))
                    last.then_inc(sP.h, 1)
                p.on("tensor", mm_gate)
                aw = sA.take(1)

                def a_silu(e, pw=pw):
                    e.wait_ge(sP.h, pw)
                    for ns, nw in nsl:
                        last = e.activation(out=bfv(F_SG + ns, [[1, nw]]),
                                            in_=ps[:, ns:ns + nw],
                                            func=AF.Silu)
                    last.then_inc(sA.h, 1)
                p.on("scalar", a_silu)
                pw2 = sP.take(1)

                def mm_up(e, t=t, aw=aw):
                    e.wait_ge(sA.h, aw)
                    for k in range(8):
                        for ns, nw in nsl:
                            last = e.matmul(
                                ps[:, ns:ns + nw],
                                bfv(F_XQT + k * CHUNK + t * 128, [[1, 128]]),
                                bfv(WUQ + k * FI + ns, [[1, nw]]),
                                start=(k == 0), stop=(k == 7))
                    last.then_inc(sP.h, 1)
                p.on("tensor", mm_up)
                up_last[0] = pw2
                gprev = sG.n
                vh_inc[0] = sV.take(1)

                def v_h(e, pw2=pw2, gprev=gprev):
                    e.wait_ge(sP.h, pw2)
                    e.wait_ge(sG.h, gprev)
                    e.tensor_tensor(out=bfv(F_H, [[1, FI]]),
                                    in0=ps[:, 0:FI],
                                    in1=bfv(F_SG, [[1, FI]]),
                                    op=ALU.mult).then_inc(sV.h, 1)
                p.on("vector", v_h)
                gpq = emit_quant_gp(bfv(F_H + 1024, [[1, 1024]]),
                              bfv(F_H + 1024, [[16, 64], [1, 16]]),
                              bfv(F_H + 1024, [[16, 64], [1, 16]]),
                              1024, nboff=64, usb=X_SB + 4096 + 3072)
                for qoff, Fs in ((0, 1024), (2048, 768)):
                    qdone = emit_quant(
                        bfv(F_H + qoff, [[1, Fs]]),
                        bfv(F_H + qoff, [[16, Fs // 16], [1, 16]]),
                        bfv(F_H + qoff, [[16, Fs // 16], [1, 16]]),
                        Fs, usb=X_SB + 4096 + 2048, nboff=qoff // 16)
                sG.take(16)

                def st(e, t=t, vq=qdone, gpq=gpq, tguard=lhqt_last[0]):
                    e.wait_ge(sV.h, vq)
                    e.wait_ge(sGC.h, gpq)
                    e.wait_ge(sT.h, tguard[0])
                    e.wait_ge(sT2.h, tguard[1])
                    e.dma_start(
                        out=hqs[t * 128:(t + 1) * 128, :],
                        in_=bfv(F_H, [[1, FI]])).then_inc(sG.h, 16)
                p.on("gpsimd", st)

            hq_done = sG.n
            vlast = vh_inc[0]
            dn_pe = []
            for cc in range(22):
                sl = cc % 4
                guard = dn_pe[cc - 4] if cc >= 4 else None
                sTq = sT if cc % 2 == 0 else sT2
                qeng = "scalar" if cc % 2 == 0 else "sync"
                tw3 = sTq.take(16)

                def l_hqt(e, cc=cc, sl=sl, guard=guard, hq_done=hq_done,
                          tw3=tw3, sTq=sTq):
                    if cc <= 1:
                        e.wait_ge(sG.h, hq_done)
                    if guard is not None:
                        e.wait_ge(sP.h, guard)
                    e.wait_ge(sTq.h, tw3 - 16)
                    e.dma_start_transpose(
                        out=bfv(F_HQT + sl * 512, [[1, 512]]),
                        in_=hqs[:, cc * 128:(cc + 1) * 128]).then_inc(sTq.h,
                                                                      16)
                p.on(qeng, l_hqt)
                pq = sP.take(1)

                def mm_dn(e, cc=cc, sl=sl, tw3=tw3, vlast=vlast, sTq=sTq):
                    if cc == 0:
                        e.wait_ge(sV.h, vlast)
                    e.wait_ge(sTq.h, tw3)
                    for t in range(TPC):
                        for half in range(2):
                            last = e.matmul(
                                ps[:, PS_DN + t * DM + half * 512:
                                   PS_DN + t * DM + half * 512 + 512],
                                bfv(F_HQT + sl * 512 + t * 128, [[1, 128]]),
                                bfv(WDQ + cc * DM + half * 512, [[1, 512]]),
                                start=(cc == 0), stop=(cc == 21))
                    last.then_inc(sP.h, 1)
                p.on("tensor", mm_dn)
                dn_pe.append(pq)
            lhqt_last[0] = (sT.n, sT2.n)
            pw = sP.n

            for t in range(TPC):
                gprev2 = sG.n
                vq = sV.take(1)

                def v_res(e, t=t, pw=pw, gprev2=gprev2, lbase=lbase):
                    e.wait_ge(sP.h, pw)
                    e.wait_ge(sG.h, gprev2)
                    e.wait_ge(sL.h, lbase)
                    e.tensor_tensor(
                        out=fv(X_SB + 4096 + (t % 2) * DM, [[1, DM]]),
                        in0=fv(X_SB + t * DM, [[1, DM]]),
                        in1=ps[:, PS_DN + t * DM:PS_DN + (t + 1) * DM],
                        op=ALU.add).then_inc(sV.h, 1)
                p.on("vector", v_res)
                sG.take(16)

                def g_st(e, c=c, t=t, vq=vq):
                    e.wait_ge(sV.h, vq)
                    e.dma_start(
                        out=out_d[c * CHUNK + t * 128:
                                  c * CHUNK + (t + 1) * 128, :],
                        in_=fv(X_SB + 4096 + (t % 2) * DM, [[1, DM]])
                    ).then_inc(sG.h, 16)
                p.on("gpsimd", g_st)
            vres2_last[0] = sV.n

        gq = sG.n
        p.on("gpsimd", lambda e, gq=gq: e.wait_ge(sG.h, gq))
        p.emit()
    return nc


def _mask01():
    jc = np.arange(W)[:, None]
    i = np.arange(W)[None, :]
    return np.where(i >= jc, 0.0, -30000.0).astype(ml_dtypes.bfloat16)


def _ident():
    return np.eye(W, dtype=ml_dtypes.bfloat16)


def kernel(L0_attn_x, L0_attn_wq, L0_attn_wkv, L0_attn_wo, L0_attn_bo,
           L0_ffn_w_gate, L0_ffn_w_up, L0_ffn_w_down):
    x = np.ascontiguousarray(np.asarray(L0_attn_x, dtype=np.float32))
    B, N, D = x.shape
    n_seq = B // NCORES
    shards = x.reshape(NCORES, n_seq * N, D)
    wq = np.ascontiguousarray(np.asarray(L0_attn_wq, dtype=np.float32))
    wkv = np.ascontiguousarray(np.asarray(L0_attn_wkv, dtype=np.float32))
    wo = np.ascontiguousarray(np.asarray(L0_attn_wo, dtype=np.float32))
    bo = np.ascontiguousarray(
        np.asarray(L0_attn_bo, dtype=np.float32)).reshape(1, DM)
    wg = np.ascontiguousarray(np.asarray(L0_ffn_w_gate, dtype=np.float32))
    wu = np.ascontiguousarray(np.asarray(L0_ffn_w_up, dtype=np.float32))
    wd = np.ascontiguousarray(np.asarray(L0_ffn_w_down, dtype=np.float32))
    m01 = _mask01()
    idn = _ident()

    nc = build_program(n_seq)
    in_maps = [{"x": shards[i], "wq": wq, "wkv": wkv, "wo": wo, "bo": bo,
                "wgate": wg, "wup": wu, "wdown": wd, "mask01": m01,
                "ident": idn}
               for i in range(NCORES)]
    res = run_bass_kernel_spmd(nc, in_maps, core_ids=list(range(NCORES)))
    out = np.concatenate([np.asarray(r["out"]) for r in res.results], axis=0)
    return out.reshape(B, N, D).astype(np.float32)


# version 18
# speedup vs baseline: 1.8561x; 1.8561x over previous
import numpy as np
import ml_dtypes
from contextlib import ExitStack

import concourse.bass as bass
import concourse.mybir as mybir
from concourse.bass_utils import run_bass_kernel_spmd

F32 = mybir.dt.float32
BF16 = mybir.dt.bfloat16
AF = mybir.ActivationFunctionType
ALU = mybir.AluOpType
AX = mybir.AxisListType

NCORES = 8
SEQ = 2048
DM = 1024
H = 16
W = 128
FI = 2816
CHUNK = 512
TPC = CHUNK // 128
WPC = CHUNK // W

WGQ = 0
WUQ = 22528
WDQ = 45056
A_WQ = 0
A_WK = 8192
A_WV = 16384
A_WO = 24576
A_XNT = 32768
A_QT = 36864
A_KT = 40960
A_VT = 46080
A_PT = 51280
A_XN1 = 52304
A_JNK = 54352
F_G = 67584
F_SGN = 70400
F_SG = 73216
F_H = 76032
F_XQT = 78848
F_HQT = 82944
F_XQ = 82944
NBF = 84992

X_SB = 0
OUTB = 8192
AM = 9216
SC = 9392
RC = 9568
LNS = 9744
RS = 9752
NF32 = 9768

PS_AVS = 0
PS_SIMA = 2048
PS_SIMB = 2560
PS_MM0 = 3072
PS_MM1 = 3584
PS_WO = 3072
PS_DN = 0
PS_QU = 3072

ENGS = ("sync", "scalar", "vector", "tensor", "gpsimd")


class Sem:
    def __init__(self, nc, es, name):
        self.h = es.enter_context(nc.semaphore(name))
        self.n = 0

    def take(self, k=1):
        self.n += k
        return self.n


class Prog:
    def __init__(self, nc):
        self.nc = nc
        self.streams = {e: [] for e in ENGS}

    def on(self, eng, fn):
        self.streams[eng].append(fn)

    def emit(self):
        with self.nc.Block() as b:
            for eng in ENGS:
                fns = self.streams[eng]
                getattr(b, eng)(lambda e, fns=fns: [f(e) for f in fns])


def build_program(n_seq):
    n_tok = n_seq * SEQ
    n_chunks = n_tok // CHUNK
    cps = SEQ // CHUNK

    nc = bass.Bass("TRN2", target_bir_lowering=False)
    x_d = nc.dram_tensor("x", [n_tok, DM], F32, kind="ExternalInput").ap()
    wq_d = nc.dram_tensor("wq", [DM, DM], F32, kind="ExternalInput").ap()
    wkv_d = nc.dram_tensor("wkv", [DM, 2 * DM], F32, kind="ExternalInput").ap()
    wo_d = nc.dram_tensor("wo", [DM, DM], F32, kind="ExternalInput").ap()
    bo_d = nc.dram_tensor("bo", [1, DM], F32, kind="ExternalInput").ap()
    wg_d = nc.dram_tensor("wgate", [FI, DM], F32, kind="ExternalInput").ap()
    wu_d = nc.dram_tensor("wup", [FI, DM], F32, kind="ExternalInput").ap()
    wd_d = nc.dram_tensor("wdown", [DM, FI], F32, kind="ExternalInput").ap()
    mask_d = nc.dram_tensor("mask01", [W, W], BF16, kind="ExternalInput").ap()
    ident_d = nc.dram_tensor("ident", [W, W], BF16, kind="ExternalInput").ap()
    out_d = nc.dram_tensor("out", [n_tok, DM], F32, kind="ExternalOutput").ap()
    xn1s = nc.dram_tensor("xn1s", [CHUNK, DM], BF16).ap()
    aos = nc.dram_tensor("aos", [CHUNK, DM], BF16).ap()
    x1s = nc.dram_tensor("x1s", [n_tok, DM], F32).ap()
    xqs = nc.dram_tensor("xqs", [n_tok, DM], BF16).ap()
    hqs = nc.dram_tensor("hqs", [CHUNK, FI], BF16).ap()
    wgqs = nc.dram_tensor("wgqs", [FI, DM], BF16).ap()
    wuqs = nc.dram_tensor("wuqs", [FI, DM], BF16).ap()
    wdqs = nc.dram_tensor("wdqs", [DM, FI], BF16).ap()

    with ExitStack() as es:
        bfar = es.enter_context(nc.sbuf_tensor("bfar", [128, NBF], BF16))
        far = es.enter_context(nc.sbuf_tensor("far", [128, NF32], F32))
        mask_sb = es.enter_context(nc.sbuf_tensor("mask_sb", [W, W], BF16))
        ident_sb = es.enter_context(nc.sbuf_tensor("ident_sb", [W, W], BF16))
        ones_c = es.enter_context(nc.sbuf_tensor("ones_c", [128, 1], BF16))
        ones_r = es.enter_context(nc.sbuf_tensor("ones_r", [1, 128], BF16))
        bo_sb = es.enter_context(nc.sbuf_tensor("bo_sb", [1, DM], BF16))
        cneg3 = es.enter_context(nc.sbuf_tensor("cneg3", [128, 1], F32))
        ceighth = es.enter_context(nc.sbuf_tensor("ceighth", [128, 1], F32))
        ps = es.enter_context(nc.psum_tensor("ps", [128, 4096], F32))

        sL = Sem(nc, es, "sL")
        sL2 = Sem(nc, es, "sL2")
        sT = Sem(nc, es, "sT")
        sT2 = Sem(nc, es, "sT2")
        sG = Sem(nc, es, "sG")
        sV = Sem(nc, es, "sV")
        sA = Sem(nc, es, "sA")
        sP = Sem(nc, es, "sP")
        sGC = Sem(nc, es, "sGC")
        bar = Sem(nc, es, "bar")

        def bfv(off, dims):
            return bass.AP(bfar, off, [[NBF, 128]] + dims)

        def fv(off, dims):
            return bass.AP(far, off, [[NF32, 128]] + dims)

        p = Prog(nc)

        def barrier():
            tgt = bar.take(5)
            waits = [(sL.h, sL.n), (sL2.h, sL2.n), (sT.h, sT.n),
                     (sT2.h, sT2.n), (sG.h, sG.n), (sV.h, sV.n),
                     (sA.h, sA.n), (sP.h, sP.n), (sGC.h, sGC.n)]
            for eng in ENGS:
                def bfn(e, waits=waits, tgt=tgt):
                    for h, n in waits:
                        e.wait_ge(h, n)
                    e.nop().then_inc(bar.h, 1)
                    e.wait_ge(bar.h, tgt)
                p.on(eng, bfn)

        def emit_ln(x_ap, out_ap, junk_ap):
            ssq = fv(LNS + 0, [[1, 1]])
            sm = fv(LNS + 1, [[1, 1]])
            mean = fv(LNS + 2, [[1, 1]])
            veps = fv(LNS + 3, [[1, 1]])
            rs0 = fv(LNS + 4, [[1, 1]])
            t1 = fv(LNS + 5, [[1, 1]])
            rstd = fv(LNS + 6, [[1, 1]])
            bias = fv(LNS + 7, [[1, 1]])
            vprev = sV.n
            aw = sA.take(1)

            def sc1(e, x_ap=x_ap, junk_ap=junk_ap, vprev=vprev):
                e.wait_ge(sV.h, vprev)
                e.activation(out=junk_ap, in_=x_ap, func=AF.Square,
                             accum_out=ssq)
                e.drain()
                e.activation(out=junk_ap, in_=x_ap, func=AF.Copy,
                             accum_out=sm).then_inc(sA.h, 1)
            p.on("scalar", sc1)
            vw = sV.take(1)

            def v1(e, x_ap=x_ap, aw=aw):
                e.wait_ge(sA.h, aw)
                e.drain()
                e.tensor_scalar(out=mean, in0=sm, scalar1=1.0 / DM,
                                scalar2=None, op0=ALU.mult)
                e.drain()
                e.tensor_tensor(out=t1, in0=mean, in1=mean, op=ALU.mult)
                e.tensor_scalar(out=veps, in0=ssq, scalar1=1.0 / DM,
                                scalar2=None, op0=ALU.mult)
                e.drain()
                e.tensor_tensor(out=veps, in0=veps, in1=t1, op=ALU.subtract)
                e.drain()
                e.tensor_scalar(out=veps, in0=veps, scalar1=1e-5,
                                scalar2=None, op0=ALU.add).then_inc(sV.h, 1)
            p.on("vector", v1)
            aw2 = sA.take(1)

            def sc2(e, vw=vw):
                e.wait_ge(sV.h, vw)
                e.activation(out=rs0, in_=veps, func=AF.Sqrt).then_inc(sA.h, 1)
            p.on("scalar", sc2)
            sV.take(1)

            def v2(e, x_ap=x_ap, out_ap=out_ap, aw2=aw2):
                e.wait_ge(sA.h, aw2)
                e.reciprocal(out=rs0, in_=rs0)
                e.drain()
                e.tensor_tensor(out=t1, in0=rs0, in1=rs0, op=ALU.mult)
                e.drain()
                e.tensor_tensor(out=t1, in0=t1, in1=veps, op=ALU.mult)
                e.drain()
                e.tensor_scalar(out=t1, in0=t1, scalar1=-0.5, scalar2=1.5,
                                op0=ALU.mult, op1=ALU.add)
                e.drain()
                e.tensor_tensor(out=rstd, in0=rs0, in1=t1, op=ALU.mult)
                e.drain()
                e.scalar_tensor_tensor(out=bias, in0=mean, scalar=-1.0,
                                       in1=rstd, op0=ALU.mult, op1=ALU.mult)
                e.drain()
                e.tensor_scalar(out=out_ap, in0=x_ap, scalar1=rstd,
                                scalar2=bias, op0=ALU.mult,
                                op1=ALU.add).then_inc(sV.h, 1)
            p.on("vector", v2)

        def emit_quant(xf2, xf3, q3, F, uoff=0, nboff=0, gguard=None,
                       usb=None):
            nb = F // 16
            am2 = fv(AM + nboff, [[1, nb]])
            am3 = fv(AM + nboff, [[1, nb], [1, 1]])
            sc = fv(SC + nboff, [[1, nb]])
            sc_b = fv(SC + nboff, [[1, nb], [0, 16]])
            rc = fv(RC + nboff, [[1, nb]])
            rc_b = fv(RC + nboff, [[1, nb], [0, 16]])
            if usb is None:
                u = bass.AP(ps, uoff, [[4096, 128], [1, F]])
                u3 = bass.AP(ps, uoff, [[4096, 128], [16, nb], [1, 16]])
            else:
                u = fv(usb, [[1, F]])
                u3 = fv(usb, [[16, nb], [1, 16]])
            g = bfv(F_G, [[1, F]])
            g3 = bfv(F_G, [[16, nb], [1, 16]])
            sgn = bfv(F_SGN, [[1, F]])
            vprev = sV.n
            aw = sA.take(1)

            def sc1(e, xf2=xf2, vprev=vprev):
                e.wait_ge(sV.h, vprev)
                e.activation(out=sgn, in_=xf2, func=AF.Sign).then_inc(sA.h, 1)
            p.on("scalar", sc1)
            sV.take(1)

            def v1(e, xf3=xf3, q3=q3, aw=aw, u=u, u3=u3, g=g, g3=g3, sgn=sgn, gguard=gguard,
                   am2=am2, am3=am3, sc=sc, sc_b=sc_b, rc=rc, rc_b=rc_b):
                if gguard is not None:
                    e.wait_ge(sG.h, gguard)
                e.drain()
                e.tensor_reduce(out=am3, in_=xf3, axis=AX.X, op=ALU.max,
                                apply_absolute_value=True)
                e.drain()
                e.tensor_scalar(out=sc, in0=am2, scalar1=1.0 / 6.0,
                                scalar2=1e-12, op0=ALU.mult, op1=ALU.max)
                e.drain()
                e.reciprocal(out=rc, in_=sc)
                e.drain()
                e.scalar_tensor_tensor(out=u, in0=xf2, scalar=-1.0, in1=xf2,
                                       op0=ALU.mult, op1=ALU.max)
                e.drain()
                e.tensor_tensor(out=u3, in0=u3, in1=rc_b, op=ALU.mult)
                e.drain()
                e.tensor_scalar(out=g, in0=u, scalar1=0.25, scalar2=None,
                                op0=ALU.is_gt)
                for t in (0.75, 1.25, 1.75):
                    e.drain()
                    e.scalar_tensor_tensor(out=g, in0=u, scalar=t, in1=g,
                                           op0=ALU.is_gt, op1=ALU.add)
                e.drain()
                e.tensor_scalar(out=g, in0=g, scalar1=0.5, scalar2=None,
                                op0=ALU.mult)
                for t in (2.5, 3.5, 5.0, 5.0):
                    e.drain()
                    e.scalar_tensor_tensor(out=g, in0=u, scalar=t,
                                           in1=g, op0=ALU.is_gt,
                                           op1=ALU.add)
                e.wait_ge(sA.h, aw)
                e.drain()
                e.tensor_tensor(out=g, in0=g, in1=sgn, op=ALU.mult)
                e.drain()
                e.tensor_tensor(out=q3, in0=g3, in1=sc_b,
                                op=ALU.mult).then_inc(sV.h, 1)
            p.on("vector", v1)
            return sV.n

        sG.take(80)

        def p0_gp(e):
            e.dma_start(
                out=bfv(A_WQ, [[DM, 8], [1, DM]]),
                in_=bass.AP(wq_d.tensor, 0,
                            [[DM, 128], [DM * 128, 8], [1, DM]])
            ).then_inc(sG.h, 16)
            e.dma_start(
                out=bfv(A_WK, [[DM, 8], [1, DM]]),
                in_=bass.AP(wkv_d.tensor, 0,
                            [[2 * DM, 128], [2 * DM * 128, 8], [1, DM]])
            ).then_inc(sG.h, 16)
            e.dma_start(
                out=bfv(A_WV, [[DM, 8], [1, DM]]),
                in_=bass.AP(wkv_d.tensor, DM,
                            [[2 * DM, 128], [2 * DM * 128, 8], [1, DM]])
            ).then_inc(sG.h, 16)
            e.dma_start(
                out=bfv(A_WO, [[DM, 8], [1, DM]]),
                in_=bass.AP(wo_d.tensor, 0,
                            [[DM, 128], [DM * 128, 8], [1, DM]])
            ).then_inc(sG.h, 16)
            e.dma_start(out=bo_sb[:], in_=bo_d).then_inc(sG.h, 16)
        p.on("gpsimd", p0_gp)
        sL.take(32)
        p.on("sync", lambda e: (e.dma_start(out=mask_sb[:],
                                            in_=mask_d).then_inc(sL.h, 16),
                                e.dma_start(out=ident_sb[:],
                                            in_=ident_d).then_inc(sL.h, 16)))
        p.on("vector", lambda e: (e.memset(ones_c[:], 1.0),
                                  e.memset(ones_r[:], 1.0),
                                  e.memset(cneg3[:], -3.0),
                                  e.memset(ceighth[:], 0.125),
                                  e.memset(bfv(A_VT + 64,
                                               [[1040, 5], [65, 16], [1, 1]]),
                                           1.0)))

        wjobs = ([(wg_d, wgqs, i, DM) for i in range(FI // 128)]
                 + [(wu_d, wuqs, i, DM) for i in range(FI // 128)]
                 + [(wd_d, wdqs, i, FI) for i in range(DM // 128)])
        jbuf = (X_SB, X_SB + 4096)
        jdone = []
        jstore = []
        for jidx, (src_d, dst_s, i, F) in enumerate(wjobs):
            nb = F // 16
            boff = jbuf[jidx % 2]
            sLx = sL if jidx % 2 == 0 else sL2
            guards = jdone[jidx - 2] if jidx >= 2 else (0, 0)
            lw = sLx.take(16)

            def ld(e, src_d=src_d, i=i, F=F, boff=boff, guards=guards,
                   sLx=sLx):
                e.wait_ge(sV.h, guards[0])
                e.wait_ge(sA.h, guards[1])
                e.dma_start(
                    out=fv(boff, [[1, F]]),
                    in_=bass.AP(src_d.tensor, i * 128 * F,
                                [[F, 128], [1, F]])).then_inc(sLx.h, 16)
            p.on("sync", ld)
            p.on("vector", lambda e, lw=lw, sLx=sLx: e.wait_ge(sLx.h, lw))
            p.on("scalar", lambda e, lw=lw, sLx=sLx: e.wait_ge(sLx.h, lw))
            qdone = emit_quant(fv(boff, [[1, F]]),
                               fv(boff, [[16, nb], [1, 16]]),
                               bfv(F_H, [[16, nb], [1, 16]]), F,
                               gguard=(jstore[-1] if jstore else None))
            jdone.append((sV.n, sA.n))
            jstore.append(sG.take(16))

            def st(e, dst_s=dst_s, i=i, F=F, vq=qdone):
                e.wait_ge(sV.h, vq)
                e.dma_start(
                    out=bass.AP(dst_s.tensor, i * 128 * F, [[F, 128], [1, F]]),
                    in_=bfv(F_H, [[1, F]])).then_inc(sG.h, 16)
            p.on("gpsimd", st)

        barrier()

        exp_bank_guard = [0, 0]
        pt_slot_guard = [0, 0]
        ce = {"tw": 0, "tw2": 0, "wo": 0, "vres": 0, "s2gq": 0,
              "carry": 0, "pdone": 0, "vn": 0}
        ce_hist = []
        lbases = {}

        def emit_s1_load(cc):
            guards = ce_hist[cc - 2] if cc >= 2 else (0, 0, 0)
            lbase = sL.take(TPC * 16)
            lbases[cc] = lbase
            xoff = X_SB + (cc % 2) * 4096

            def s1_load(e, cc=cc, guards=guards, xoff=xoff, lbase=lbase):
                e.wait_ge(sL.h, lbase - TPC * 16)
                e.wait_ge(sV.h, guards[0])
                e.wait_ge(sA.h, guards[1])
                e.wait_ge(sG.h, guards[2])
                for t in range(TPC):
                    e.dma_start(
                        out=fv(xoff + t * DM, [[1, DM]]),
                        in_=x_d[cc * CHUNK + t * 128:
                                cc * CHUNK + (t + 1) * 128, :]
                    ).then_inc(sL.h, 16)
            p.on("sync", s1_load)

        def emit_ln1(cc):
            xoff = X_SB + (cc % 2) * 4096
            lbase = lbases[cc]
            for t in range(TPC):
                gprev = sG.n
                p.on("vector", lambda e, lbase=lbase, gprev=gprev:
                     (e.wait_ge(sL.h, lbase), e.wait_ge(sG.h, gprev)))
                p.on("scalar", lambda e, lbase=lbase: e.wait_ge(sL.h, lbase))
                emit_ln(fv(xoff + t * DM, [[1, DM]]),
                        bfv(A_XN1 + (t % 2) * DM, [[1, DM]]),
                        bfv(A_JNK, [[1, DM]]))
                vq = sV.n
                sG.take(16)

                def st(e, t=t, vq=vq, tg=ce["tw"]):
                    e.wait_ge(sV.h, vq)
                    e.wait_ge(sT.h, tg)
                    e.dma_start(
                        out=xn1s[t * 128:(t + 1) * 128, :],
                        in_=bfv(A_XN1 + (t % 2) * DM, [[1, DM]])
                    ).then_inc(sG.h, 16)
                p.on("gpsimd", st)
            ce["s2gq"] = sG.n

        def emit_s2(cc):
            tpre1 = sT.n
            tw = sT.take(8 * 16)
            ce["tw"] = tw

            def s2(e, gq=ce["s2gq"], wg=ce["wo"], tpre1=tpre1):
                e.wait_ge(sG.h, gq)
                e.wait_ge(sP.h, wg)
                e.wait_ge(sT.h, tpre1)
                for k in range(8):
                    e.dma_start_transpose(
                        out=bfv(A_XNT + k * CHUNK, [[1, CHUNK]]),
                        in_=xn1s[:, k * 128:(k + 1) * 128]).then_inc(sT.h, 16)
            p.on("sync", s2)
            return tw

        def emit_carry(cc):
            new_seq = (cc % cps == 0)
            carry_done = sV.take(1)
            ce["carry"] = carry_done

            def s3_carry(e, new_seq=new_seq, pdone=ce["pdone"]):
                e.wait_ge(sA.h, pdone)
                if new_seq:
                    e.memset(bfv(A_KT, [[640, 8], [1, 128]]), 0.0)
                    e.drain()
                    e.memset(bfv(A_VT, [[65, 16], [1, 64]]), 0.0)
                else:
                    e.tensor_copy(out=bfv(A_KT, [[640, 8], [1, 128]]),
                                  in_=bfv(A_KT + 4 * 128, [[640, 8], [1, 128]]))
                    e.drain()
                    e.tensor_copy(out=bfv(A_VT, [[65, 16], [1, 64]]),
                                  in_=bfv(A_VT + 4 * 1040, [[65, 16], [1, 64]]))
                e.engine_nop().then_inc(sV.h, 1)
            p.on("vector", s3_carry)

        def make_projs(cc, tw):
            jobs = []
            for woff, base, pitch in ((A_WQ, A_QT, 512), (A_WK, A_KT, 640)):
                for m in range(8):
                    dsto = base + m * pitch + (128 if base == A_KT else 0)
                    jobs.append((
                        (lambda k, woff=woff, m=m:
                         bfv(woff + k * DM + m * 128, [[1, 128]])),
                        (lambda k: bfv(A_XNT + k * CHUNK, [[1, CHUNK]])),
                        dsto))
            for t in range(TPC):
                for half in range(2):
                    jobs.append((
                        (lambda k, t=t: bfv(A_XNT + k * CHUNK + t * 128,
                                            [[1, 128]])),
                        (lambda k, half=half: bfv(A_WV + k * DM + half * 512,
                                                  [[1, 512]])),
                        (A_VT + (1 + t) * 1040 + half * 520, "v")))
            state = {"slot": 0, "first": True}
            evac_hist = []

            def emit_group(lo, hi, tw=tw):
                for j in range(lo, hi):
                    lhs_fn, rhs_fn, dst_off = jobs[j]
                    col = PS_MM0 if state["slot"] == 0 else PS_MM1
                    bank_guard = (evac_hist[-2] if len(evac_hist) >= 2
                                  else None)
                    pw = sP.take(1)
                    first = state["first"]
                    state["first"] = False

                    def mm(e, col=col, bank_guard=bank_guard, first=first,
                           lhs_fn=lhs_fn, rhs_fn=rhs_fn, tw=tw,
                           carry_done=ce["carry"], vres_g=ce["vres"]):
                        if first:
                            e.wait_ge(sT.h, tw)
                            e.wait_ge(sV.h, carry_done)
                            e.wait_ge(sV.h, vres_g)
                        if bank_guard is not None:
                            e.wait_ge(sA.h, bank_guard)
                        for k in range(8):
                            last = e.matmul(ps[:, col:col + 512], lhs_fn(k),
                                            rhs_fn(k), start=(k == 0),
                                            stop=(k == 7))
                        last.then_inc(sP.h, 1)
                    p.on("tensor", mm)
                    ev_n = sA.take(1)

                    def ev(e, col=col, dst_off=dst_off, pw=pw):
                        e.wait_ge(sP.h, pw)
                        if isinstance(dst_off, tuple):
                            dst = bfv(dst_off[0], [[65, 8], [1, 64]])
                            src = bass.AP(ps, col,
                                          [[4096, 128], [64, 8], [1, 64]])
                        else:
                            dst = bfv(dst_off, [[1, 512]])
                            src = ps[:, col:col + 512]
                        e.activation(out=dst, in_=src,
                                     func=AF.Copy).then_inc(sA.h, 1)
                    p.on("scalar", ev)
                    evac_hist.append(ev_n)
                    state["slot"] ^= 1
                if hi == len(jobs):
                    ce["pdone"] = sA.n
            return emit_group

        def emit_windows(cc):
            aos_gs = []
            for w in range(WPC):
                norm_prev = ce["vn"] if w == 0 else sV.n
                sim_incs = {}

                def emit_sim(pp, w=w, norm_prev=norm_prev,
                             proj_done=ce["pdone"]):
                    pw = sP.take(1)
                    sim_incs[pp] = pw
                    guard = exp_bank_guard[pp % 2]

                    def mm_sim(e, p=pp, w=w, guard=guard,
                               norm_prev=norm_prev, proj_done=proj_done):
                        if p == 0:
                            e.wait_ge(sA.h, proj_done)
                        if guard:
                            e.wait_ge(sA.h, guard)
                        simc = PS_SIMA + (p % 2) * 1024
                        for j2 in range(2):
                            sub = j2 * 64
                            for j, sl in enumerate((w, w + 1)):
                                e.matmul(
                                    ps[:, simc + j2 * 512 + j * 128:
                                       simc + j2 * 512 + j * 128 + 128],
                                    bass.AP(bfar, sub * NBF + A_KT + p * 640
                                            + sl * 128,
                                            [[NBF, 64], [1, 128]]),
                                    bass.AP(bfar, sub * NBF + A_QT + p * 512
                                            + w * 128, [[NBF, 64], [1, 128]]),
                                    start=True, stop=(j == 0))
                            last = e.matmul(
                                ps[:, simc + j2 * 512 + 128:
                                   simc + j2 * 512 + 256],
                                ident_sb[:], mask_sb[:],
                                start=False, stop=True)
                        last.then_inc(sP.h, 1)
                    p.on("tensor", mm_sim)

                emit_sim(0)
                emit_sim(1)
                for p0 in range(8):
                    simc = PS_SIMA + (p0 % 2) * 1024
                    pt_off = A_PT + (p0 % 2) * 512
                    aw = sA.take(1)
                    exp_bank_guard[p0 % 2] = aw
                    slot_guard = pt_slot_guard[p0 % 2]

                    def act_exp(e, pt_off=pt_off, simc=simc,
                                pw=sim_incs[p0], slot_guard=slot_guard):
                        e.wait_ge(sP.h, pw)
                        if slot_guard:
                            e.wait_ge(sP.h, slot_guard)
                        e.activation(out=bfv(pt_off, [[1, 512]]),
                                     in_=bass.AP(ps, simc,
                                                 [[4096, 128], [512, 2],
                                                  [1, 256]]),
                                     func=AF.Exp, bias=cneg3[:],
                                     scale=ceighth[:]).then_inc(sA.h, 1)
                    p.on("scalar", act_exp)
                    pw2 = sP.take(1)
                    pt_slot_guard[p0 % 2] = pw2

                    def mm_av(e, pt_off=pt_off, w=w, p=p0, aw=aw,
                              norm_prev=norm_prev):
                        e.wait_ge(sA.h, aw)
                        if p == 0:
                            e.wait_ge(sV.h, norm_prev)
                        for j2 in range(2):
                            h = 2 * p + j2
                            for j, sl in enumerate((w, w + 1)):
                                lhsT = bfv(pt_off + j2 * 256 + j * 128,
                                           [[1, 128]])
                                last = e.matmul(
                                    ps[:, PS_AVS + h * 128:
                                       PS_AVS + h * 128 + 65],
                                    lhsT,
                                    bfv(A_VT + sl * 1040 + h * 65, [[1, 65]]),
                                    start=(j == 0), stop=(j == 1))
                        last.then_inc(sP.h, 1)
                    p.on("tensor", mm_av)
                    if p0 + 2 < 8:
                        emit_sim(p0 + 2)
                pw = sP.n
                gprev = sG.n
                vq = sV.take(1)

                def v_norm(e, w=w, pw=pw, gprev=gprev):
                    e.wait_ge(sP.h, pw)
                    e.wait_ge(sG.h, gprev)
                    e.reciprocal(out=fv(RS, [[1, H]]),
                                 in_=bass.AP(ps, 64, [[4096, 128], [128, 16],
                                                      [1, 1]]))
                    e.drain()
                    e.scalar_tensor_tensor(
                        out=bfv(A_XN1 + (w % 2) * DM, [[1, DM]]),
                        in0=bass.AP(ps, 0, [[4096, 128], [128, 16], [1, 64]]),
                        scalar=1.0,
                        in1=fv(RS, [[1, H], [0, 64]]),
                        op0=ALU.mult, op1=ALU.mult).then_inc(sV.h, 1)
                p.on("vector", v_norm)
                sG.take(16)
                aos_gs.append(sG.n)

                def g_st(e, w=w, vq=vq, tg=ce["tw2"]):
                    e.wait_ge(sV.h, vq)
                    e.wait_ge(sT.h, tg)
                    e.dma_start(
                        out=aos[w * 128:(w + 1) * 128, :],
                        in_=bfv(A_XN1 + (w % 2) * DM, [[1, DM]])
                    ).then_inc(sG.h, 16)
                p.on("gpsimd", g_st)

            ce["vn"] = vq
            gq = sG.n
            tpre2 = sT.n
            tw2 = sT.take(8 * 16)
            ce["tw2"] = tw2

            def s5(e, gq=gq, tpre2=tpre2):
                e.wait_ge(sG.h, gq)
                e.wait_ge(sT.h, tpre2)
                for k in range(8):
                    e.dma_start_transpose(
                        out=bfv(A_XNT + k * CHUNK, [[1, CHUNK]]),
                        in_=aos[:, k * 128:(k + 1) * 128]).then_inc(sT.h, 16)
            p.on("sync", s5)
            return tw2

        def emit_tail(cc, tw2):
            xoff = X_SB + (cc % 2) * 4096
            has_next = (cc + 1 < n_chunks)
            wo_pws = []
            vres_vqs = []
            for t in range(TPC):
                vprev = vres_vqs[t - 1] if t > 0 else 0
                pw = sP.take(1)
                wo_pws.append(pw)

                def mm_wo(e, t=t, vprev=vprev, tw2=tw2):
                    e.wait_ge(sT.h, tw2)
                    if t > 0:
                        e.wait_ge(sV.h, vprev)
                    for k in range(8):
                        for half in range(2):
                            e.matmul(
                                ps[:, PS_WO + half * 512:
                                   PS_WO + half * 512 + 512],
                                bfv(A_XNT + k * CHUNK + t * 128, [[1, 128]]),
                                bfv(A_WO + k * DM + half * 512, [[1, 512]]),
                                start=(k == 0), stop=False)
                    for half in range(2):
                        last = e.matmul(ps[:, PS_WO + half * 512:
                                           PS_WO + half * 512 + 512],
                                        ones_r[:],
                                        bo_sb[:, half * 512:(half + 1) * 512],
                                        start=False, stop=True)
                    last.then_inc(sP.h, 1)
                p.on("tensor", mm_wo)
                vq = sV.take(1)
                vres_vqs.append(vq)

                def v_res(e, t=t, pw=pw, xoff=xoff):
                    e.wait_ge(sP.h, pw)
                    e.tensor_tensor(out=fv(xoff + t * DM, [[1, DM]]),
                                    in0=fv(xoff + t * DM, [[1, DM]]),
                                    in1=ps[:, PS_WO:PS_WO + DM],
                                    op=ALU.add).then_inc(sV.h, 1)
                p.on("vector", v_res)
                sG.take(16)

                def g_st(e, cc=cc, t=t, vq=vq, xoff=xoff):
                    e.wait_ge(sV.h, vq)
                    e.dma_start(
                        out=x1s[cc * CHUNK + t * 128:
                                cc * CHUNK + (t + 1) * 128, :],
                        in_=fv(xoff + t * DM, [[1, DM]])).then_inc(sG.h, 16)
                p.on("gpsimd", g_st)
            ce["wo"] = sP.n
            ce["vres"] = vres_vqs[-1]
            emit_pg = None
            if has_next:
                emit_carry(cc + 1)
                emit_ln1(cc + 1)
            for t in range(TPC):
                emit_ln(fv(xoff + t * DM, [[1, DM]]), fv(OUTB, [[1, DM]]),
                        bfv(A_JNK, [[1, DM]]))
                if t == 0 and has_next:
                    tw_n = emit_s2(cc + 1)
                    emit_pg = make_projs(cc + 1, tw_n)
                qdone = emit_quant(fv(OUTB, [[1, DM]]),
                                   fv(OUTB, [[16, 64], [1, 16]]),
                                   bfv(F_XQ, [[16, 64], [1, 16]]), DM,
                                   usb=xoff + t * DM, gguard=sG.n)
                sG.take(16)

                def g_st2(e, cc=cc, t=t, vq2=qdone):
                    e.wait_ge(sV.h, vq2)
                    e.dma_start(
                        out=xqs[cc * CHUNK + t * 128:
                                cc * CHUNK + (t + 1) * 128, :],
                        in_=bfv(F_XQ, [[1, DM]])).then_inc(sG.h, 16)
                p.on("gpsimd", g_st2)
                if emit_pg is not None:
                    emit_pg(t * 6, (t + 1) * 6)
            ce_hist.append((sV.n, sA.n, sG.n))

        emit_s1_load(0)
        for c in range(n_chunks):
            if c + 1 < n_chunks:
                emit_s1_load(c + 1)
            if c == 0:
                emit_carry(0)
                emit_ln1(0)
                tw0 = emit_s2(0)
                pg0 = make_projs(0, tw0)
                pg0(0, 24)
            tw2 = emit_windows(c)
            emit_tail(c, tw2)

        barrier()
        sT.take(38 * 16)

        def p2_w(e):
            for k in range(8):
                e.dma_start_transpose(
                    out=bfv(WGQ + k * FI, [[1, FI]]),
                    in_=wgqs[:, k * 128:(k + 1) * 128]).then_inc(sT.h, 16)
            for k in range(8):
                e.dma_start_transpose(
                    out=bfv(WUQ + k * FI, [[1, FI]]),
                    in_=wuqs[:, k * 128:(k + 1) * 128]).then_inc(sT.h, 16)
            for cc in range(22):
                e.dma_start_transpose(
                    out=bfv(WDQ + cc * DM, [[1, DM]]),
                    in_=wdqs[:, cc * 128:(cc + 1) * 128]).then_inc(sT.h, 16)
        p.on("scalar", p2_w)
        barrier()

        nsl = [(i * 512, min(512, FI - i * 512))
               for i in range((FI + 511) // 512)]
        vres2_last = [sV.n]
        up_last = [0]
        lhqt_last = [(sT.n, sT2.n)]
        for c in range(n_chunks):
            lbase = sL.take(TPC * 16)

            def l_x1(e, c=c, vguard=vres2_last[0]):
                e.wait_ge(sV.h, vguard)
                for t in range(TPC):
                    e.dma_start(
                        out=fv(X_SB + t * DM, [[1, DM]]),
                        in_=x1s[c * CHUNK + t * 128:
                                c * CHUNK + (t + 1) * 128, :]
                    ).then_inc(sL.h, 16)
            p.on("sync", l_x1)

            tpre = sT.n
            tw = sT.take(8 * 16)

            def l_xqt(e, c=c, pguard=up_last[0], tpre=tpre):
                e.wait_ge(sP.h, pguard)
                e.wait_ge(sT.h, tpre)
                for k in range(8):
                    e.dma_start_transpose(
                        out=bfv(F_XQT + k * CHUNK, [[1, CHUNK]]),
                        in_=xqs[c * CHUNK:(c + 1) * CHUNK,
                                k * 128:(k + 1) * 128]).then_inc(sT.h, 16)
            p.on("scalar", l_xqt)

            vh_inc = [vres2_last[0]]
            for t in range(TPC):
                vprev = vh_inc[0]
                pw = sP.take(1)

                def mm_gate(e, t=t, vprev=vprev, tw=tw):
                    e.wait_ge(sT.h, tw)
                    e.wait_ge(sV.h, vprev)
                    for k in range(8):
                        for ns, nw in nsl:
                            last = e.matmul(
                                ps[:, ns:ns + nw],
                                bfv(F_XQT + k * CHUNK + t * 128, [[1, 128]]),
                                bfv(WGQ + k * FI + ns, [[1, nw]]),
                                start=(k == 0), stop=(k == 7))
                    last.then_inc(sP.h, 1)
                p.on("tensor", mm_gate)
                aw = sA.take(1)

                def a_silu(e, pw=pw):
                    e.wait_ge(sP.h, pw)
                    for ns, nw in nsl:
                        last = e.activation(out=bfv(F_SG + ns, [[1, nw]]),
                                            in_=ps[:, ns:ns + nw],
                                            func=AF.Silu)
                    last.then_inc(sA.h, 1)
                p.on("scalar", a_silu)
                pw2 = sP.take(1)

                def mm_up(e, t=t, aw=aw):
                    e.wait_ge(sA.h, aw)
                    for k in range(8):
                        for ns, nw in nsl:
                            last = e.matmul(
                                ps[:, ns:ns + nw],
                                bfv(F_XQT + k * CHUNK + t * 128, [[1, 128]]),
                                bfv(WUQ + k * FI + ns, [[1, nw]]),
                                start=(k == 0), stop=(k == 7))
                    last.then_inc(sP.h, 1)
                p.on("tensor", mm_up)
                up_last[0] = pw2
                gprev = sG.n
                vh_inc[0] = sV.take(1)

                def v_h(e, pw2=pw2, gprev=gprev):
                    e.wait_ge(sP.h, pw2)
                    e.wait_ge(sG.h, gprev)
                    e.tensor_tensor(out=bfv(F_H, [[1, FI]]),
                                    in0=ps[:, 0:FI],
                                    in1=bfv(F_SG, [[1, FI]]),
                                    op=ALU.mult).then_inc(sV.h, 1)
                p.on("vector", v_h)
                for qoff, Fs in ((0, 1024), (1024, 1024), (2048, 768)):
                    qdone = emit_quant(
                        bfv(F_H + qoff, [[1, Fs]]),
                        bfv(F_H + qoff, [[16, Fs // 16], [1, 16]]),
                        bfv(F_H + qoff, [[16, Fs // 16], [1, 16]]),
                        Fs, usb=X_SB + 4096 + 2048, nboff=qoff // 16)
                sG.take(16)

                def st(e, t=t, vq=qdone, tguard=lhqt_last[0]):
                    e.wait_ge(sV.h, vq)
                    e.wait_ge(sT.h, tguard[0])
                    e.wait_ge(sT2.h, tguard[1])
                    e.dma_start(
                        out=hqs[t * 128:(t + 1) * 128, :],
                        in_=bfv(F_H, [[1, FI]])).then_inc(sG.h, 16)
                p.on("gpsimd", st)

            hq_done = sG.n
            vlast = vh_inc[0]
            dn_pe = []
            for cc in range(22):
                sl = cc % 4
                guard = dn_pe[cc - 4] if cc >= 4 else None
                sTq = sT if cc % 2 == 0 else sT2
                qeng = "scalar" if cc % 2 == 0 else "sync"
                tw3 = sTq.take(16)

                def l_hqt(e, cc=cc, sl=sl, guard=guard, hq_done=hq_done,
                          tw3=tw3, sTq=sTq):
                    if cc <= 1:
                        e.wait_ge(sG.h, hq_done)
                    if guard is not None:
                        e.wait_ge(sP.h, guard)
                    e.wait_ge(sTq.h, tw3 - 16)
                    e.dma_start_transpose(
                        out=bfv(F_HQT + sl * 512, [[1, 512]]),
                        in_=hqs[:, cc * 128:(cc + 1) * 128]).then_inc(sTq.h,
                                                                      16)
                p.on(qeng, l_hqt)
                pq = sP.take(1)

                def mm_dn(e, cc=cc, sl=sl, tw3=tw3, vlast=vlast, sTq=sTq):
                    if cc == 0:
                        e.wait_ge(sV.h, vlast)
                    e.wait_ge(sTq.h, tw3)
                    for t in range(TPC):
                        for half in range(2):
                            last = e.matmul(
                                ps[:, PS_DN + t * DM + half * 512:
                                   PS_DN + t * DM + half * 512 + 512],
                                bfv(F_HQT + sl * 512 + t * 128, [[1, 128]]),
                                bfv(WDQ + cc * DM + half * 512, [[1, 512]]),
                                start=(cc == 0), stop=(cc == 21))
                    last.then_inc(sP.h, 1)
                p.on("tensor", mm_dn)
                dn_pe.append(pq)
            lhqt_last[0] = (sT.n, sT2.n)
            pw = sP.n

            for t in range(TPC):
                gprev2 = sG.n
                vq = sV.take(1)

                def v_res(e, t=t, pw=pw, gprev2=gprev2, lbase=lbase):
                    e.wait_ge(sP.h, pw)
                    e.wait_ge(sG.h, gprev2)
                    e.wait_ge(sL.h, lbase)
                    e.tensor_tensor(
                        out=fv(X_SB + 4096 + (t % 2) * DM, [[1, DM]]),
                        in0=fv(X_SB + t * DM, [[1, DM]]),
                        in1=ps[:, PS_DN + t * DM:PS_DN + (t + 1) * DM],
                        op=ALU.add).then_inc(sV.h, 1)
                p.on("vector", v_res)
                sG.take(16)

                def g_st(e, c=c, t=t, vq=vq):
                    e.wait_ge(sV.h, vq)
                    e.dma_start(
                        out=out_d[c * CHUNK + t * 128:
                                  c * CHUNK + (t + 1) * 128, :],
                        in_=fv(X_SB + 4096 + (t % 2) * DM, [[1, DM]])
                    ).then_inc(sG.h, 16)
                p.on("gpsimd", g_st)
            vres2_last[0] = sV.n

        gq = sG.n
        p.on("gpsimd", lambda e, gq=gq: e.wait_ge(sG.h, gq))
        p.emit()
    return nc


def _mask01():
    jc = np.arange(W)[:, None]
    i = np.arange(W)[None, :]
    return np.where(i >= jc, 0.0, -30000.0).astype(ml_dtypes.bfloat16)


def _ident():
    return np.eye(W, dtype=ml_dtypes.bfloat16)


def kernel(L0_attn_x, L0_attn_wq, L0_attn_wkv, L0_attn_wo, L0_attn_bo,
           L0_ffn_w_gate, L0_ffn_w_up, L0_ffn_w_down):
    x = np.ascontiguousarray(np.asarray(L0_attn_x, dtype=np.float32))
    B, N, D = x.shape
    n_seq = B // NCORES
    shards = x.reshape(NCORES, n_seq * N, D)
    wq = np.ascontiguousarray(np.asarray(L0_attn_wq, dtype=np.float32))
    wkv = np.ascontiguousarray(np.asarray(L0_attn_wkv, dtype=np.float32))
    wo = np.ascontiguousarray(np.asarray(L0_attn_wo, dtype=np.float32))
    bo = np.ascontiguousarray(
        np.asarray(L0_attn_bo, dtype=np.float32)).reshape(1, DM)
    wg = np.ascontiguousarray(np.asarray(L0_ffn_w_gate, dtype=np.float32))
    wu = np.ascontiguousarray(np.asarray(L0_ffn_w_up, dtype=np.float32))
    wd = np.ascontiguousarray(np.asarray(L0_ffn_w_down, dtype=np.float32))
    m01 = _mask01()
    idn = _ident()

    nc = build_program(n_seq)
    in_maps = [{"x": shards[i], "wq": wq, "wkv": wkv, "wo": wo, "bo": bo,
                "wgate": wg, "wup": wu, "wdown": wd, "mask01": m01,
                "ident": idn}
               for i in range(NCORES)]
    res = run_bass_kernel_spmd(nc, in_maps, core_ids=list(range(NCORES)))
    out = np.concatenate([np.asarray(r["out"]) for r in res.results], axis=0)
    return out.reshape(B, N, D).astype(np.float32)


# version 19
# speedup vs baseline: 1.9254x; 1.0373x over previous
import numpy as np
import ml_dtypes
from contextlib import ExitStack

import concourse.bass as bass
import concourse.mybir as mybir
from concourse.bass_utils import run_bass_kernel_spmd

F32 = mybir.dt.float32
BF16 = mybir.dt.bfloat16
AF = mybir.ActivationFunctionType
ALU = mybir.AluOpType
AX = mybir.AxisListType

NCORES = 8
SEQ = 2048
DM = 1024
H = 16
W = 128
FI = 2816
CHUNK = 512
TPC = CHUNK // 128
WPC = CHUNK // W

WGQ = 0
WUQ = 22528
WDQ = 45056
A_WQ = 0
A_WK = 8192
A_WV = 16384
A_WO = 24576
A_XNT = 32768
A_QT = 36864
A_KT = 40960
A_VT = 46080
A_PT = 51280
A_XN1 = 52304
A_JNK = 54352
F_G = 67584
F_SGN = 70400
F_SG = 73216
F_H = 76032
F_XQT = 78848
F_HQT = 82944
F_XQ = 82944
NBF = 84992

X_SB = 0
OUTB = 8192
AM = 9216
SC = 9392
RC = 9568
LNS = 9744
RS = 9752
NF32 = 9768

PS_AVS = 0
PS_SIMA = 2048
PS_SIMB = 2560
PS_MM0 = 3072
PS_MM1 = 3584
PS_WO = 3072
PS_DN = 0
PS_QU = 3072

ENGS = ("sync", "scalar", "vector", "tensor", "gpsimd")


class Sem:
    def __init__(self, nc, es, name):
        self.h = es.enter_context(nc.semaphore(name))
        self.n = 0

    def take(self, k=1):
        self.n += k
        return self.n


class Prog:
    def __init__(self, nc):
        self.nc = nc
        self.streams = {e: [] for e in ENGS}

    def on(self, eng, fn):
        self.streams[eng].append(fn)

    def emit(self):
        with self.nc.Block() as b:
            for eng in ENGS:
                fns = self.streams[eng]
                getattr(b, eng)(lambda e, fns=fns: [f(e) for f in fns])


def build_program(n_seq):
    n_tok = n_seq * SEQ
    n_chunks = n_tok // CHUNK
    cps = SEQ // CHUNK

    nc = bass.Bass("TRN2", target_bir_lowering=False)
    x_d = nc.dram_tensor("x", [n_tok, DM], F32, kind="ExternalInput").ap()
    wq_d = nc.dram_tensor("wq", [DM, DM], F32, kind="ExternalInput").ap()
    wkv_d = nc.dram_tensor("wkv", [DM, 2 * DM], F32, kind="ExternalInput").ap()
    wo_d = nc.dram_tensor("wo", [DM, DM], F32, kind="ExternalInput").ap()
    bo_d = nc.dram_tensor("bo", [1, DM], F32, kind="ExternalInput").ap()
    wg_d = nc.dram_tensor("wgate", [FI, DM], F32, kind="ExternalInput").ap()
    wu_d = nc.dram_tensor("wup", [FI, DM], F32, kind="ExternalInput").ap()
    wd_d = nc.dram_tensor("wdown", [DM, FI], F32, kind="ExternalInput").ap()
    mask_d = nc.dram_tensor("mask01", [W, W], BF16, kind="ExternalInput").ap()
    ident_d = nc.dram_tensor("ident", [W, W], BF16, kind="ExternalInput").ap()
    out_d = nc.dram_tensor("out", [n_tok, DM], F32, kind="ExternalOutput").ap()
    xn1s = nc.dram_tensor("xn1s", [CHUNK, DM], BF16).ap()
    aos = nc.dram_tensor("aos", [CHUNK, DM], BF16).ap()
    x1s = nc.dram_tensor("x1s", [n_tok, DM], F32).ap()
    xqs = nc.dram_tensor("xqs", [n_tok, DM], BF16).ap()
    hqs = nc.dram_tensor("hqs", [CHUNK, FI], BF16).ap()
    wgqs = nc.dram_tensor("wgqs", [FI, DM], BF16).ap()
    wuqs = nc.dram_tensor("wuqs", [FI, DM], BF16).ap()
    wdqs = nc.dram_tensor("wdqs", [DM, FI], BF16).ap()

    with ExitStack() as es:
        bfar = es.enter_context(nc.sbuf_tensor("bfar", [128, NBF], BF16))
        far = es.enter_context(nc.sbuf_tensor("far", [128, NF32], F32))
        mask_sb = es.enter_context(nc.sbuf_tensor("mask_sb", [W, W], BF16))
        ident_sb = es.enter_context(nc.sbuf_tensor("ident_sb", [W, W], BF16))
        ones_c = es.enter_context(nc.sbuf_tensor("ones_c", [128, 1], BF16))
        ones_r = es.enter_context(nc.sbuf_tensor("ones_r", [1, 128], BF16))
        bo_sb = es.enter_context(nc.sbuf_tensor("bo_sb", [1, DM], BF16))
        cneg3 = es.enter_context(nc.sbuf_tensor("cneg3", [128, 1], F32))
        ceighth = es.enter_context(nc.sbuf_tensor("ceighth", [128, 1], F32))
        ps = es.enter_context(nc.psum_tensor("ps", [128, 4096], F32))

        sL = Sem(nc, es, "sL")
        sL2 = Sem(nc, es, "sL2")
        sT = Sem(nc, es, "sT")
        sT2 = Sem(nc, es, "sT2")
        sG = Sem(nc, es, "sG")
        sV = Sem(nc, es, "sV")
        sA = Sem(nc, es, "sA")
        sP = Sem(nc, es, "sP")
        sGC = Sem(nc, es, "sGC")
        bar = Sem(nc, es, "bar")

        def bfv(off, dims):
            return bass.AP(bfar, off, [[NBF, 128]] + dims)

        def fv(off, dims):
            return bass.AP(far, off, [[NF32, 128]] + dims)

        p = Prog(nc)

        def barrier():
            tgt = bar.take(5)
            waits = [(sL.h, sL.n), (sL2.h, sL2.n), (sT.h, sT.n),
                     (sT2.h, sT2.n), (sG.h, sG.n), (sV.h, sV.n),
                     (sA.h, sA.n), (sP.h, sP.n), (sGC.h, sGC.n)]
            for eng in ENGS:
                def bfn(e, waits=waits, tgt=tgt):
                    for h, n in waits:
                        e.wait_ge(h, n)
                    e.nop().then_inc(bar.h, 1)
                    e.wait_ge(bar.h, tgt)
                p.on(eng, bfn)

        def emit_ln(x_ap, out_ap, junk_ap):
            ssq = fv(LNS + 0, [[1, 1]])
            sm = fv(LNS + 1, [[1, 1]])
            mean = fv(LNS + 2, [[1, 1]])
            veps = fv(LNS + 3, [[1, 1]])
            rs0 = fv(LNS + 4, [[1, 1]])
            t1 = fv(LNS + 5, [[1, 1]])
            rstd = fv(LNS + 6, [[1, 1]])
            bias = fv(LNS + 7, [[1, 1]])
            vprev = sV.n
            aw = sA.take(1)

            def sc1(e, x_ap=x_ap, junk_ap=junk_ap, vprev=vprev):
                e.wait_ge(sV.h, vprev)
                e.activation(out=junk_ap, in_=x_ap, func=AF.Square,
                             accum_out=ssq)
                e.drain()
                e.activation(out=junk_ap, in_=x_ap, func=AF.Copy,
                             accum_out=sm).then_inc(sA.h, 1)
            p.on("scalar", sc1)
            vw = sV.take(1)

            def v1(e, x_ap=x_ap, aw=aw):
                e.wait_ge(sA.h, aw)
                e.drain()
                e.tensor_scalar(out=mean, in0=sm, scalar1=1.0 / DM,
                                scalar2=None, op0=ALU.mult)
                e.drain()
                e.tensor_tensor(out=t1, in0=mean, in1=mean, op=ALU.mult)
                e.tensor_scalar(out=veps, in0=ssq, scalar1=1.0 / DM,
                                scalar2=None, op0=ALU.mult)
                e.drain()
                e.tensor_tensor(out=veps, in0=veps, in1=t1, op=ALU.subtract)
                e.drain()
                e.tensor_scalar(out=veps, in0=veps, scalar1=1e-5,
                                scalar2=None, op0=ALU.add).then_inc(sV.h, 1)
            p.on("vector", v1)
            aw2 = sA.take(1)

            def sc2(e, vw=vw):
                e.wait_ge(sV.h, vw)
                e.activation(out=rs0, in_=veps, func=AF.Sqrt).then_inc(sA.h, 1)
            p.on("scalar", sc2)
            sV.take(1)

            def v2(e, x_ap=x_ap, out_ap=out_ap, aw2=aw2):
                e.wait_ge(sA.h, aw2)
                e.reciprocal(out=rs0, in_=rs0)
                e.drain()
                e.tensor_tensor(out=t1, in0=rs0, in1=rs0, op=ALU.mult)
                e.drain()
                e.tensor_tensor(out=t1, in0=t1, in1=veps, op=ALU.mult)
                e.drain()
                e.tensor_scalar(out=t1, in0=t1, scalar1=-0.5, scalar2=1.5,
                                op0=ALU.mult, op1=ALU.add)
                e.drain()
                e.tensor_tensor(out=rstd, in0=rs0, in1=t1, op=ALU.mult)
                e.drain()
                e.scalar_tensor_tensor(out=bias, in0=mean, scalar=-1.0,
                                       in1=rstd, op0=ALU.mult, op1=ALU.mult)
                e.drain()
                e.tensor_scalar(out=out_ap, in0=x_ap, scalar1=rstd,
                                scalar2=bias, op0=ALU.mult,
                                op1=ALU.add).then_inc(sV.h, 1)
            p.on("vector", v2)

        def emit_quant(xf2, xf3, q3, F, uoff=0, nboff=0, gguard=None,
                       usb=None):
            nb = F // 16
            am2 = fv(AM + nboff, [[1, nb]])
            am3 = fv(AM + nboff, [[1, nb], [1, 1]])
            sc = fv(SC + nboff, [[1, nb]])
            sc_b = fv(SC + nboff, [[1, nb], [0, 16]])
            rc = fv(RC + nboff, [[1, nb]])
            rc_b = fv(RC + nboff, [[1, nb], [0, 16]])
            if usb is None:
                u = bass.AP(ps, uoff, [[4096, 128], [1, F]])
                u3 = bass.AP(ps, uoff, [[4096, 128], [16, nb], [1, 16]])
            else:
                u = fv(usb, [[1, F]])
                u3 = fv(usb, [[16, nb], [1, 16]])
            g = bfv(F_G, [[1, F]])
            g3 = bfv(F_G, [[16, nb], [1, 16]])
            sgn = bfv(F_SGN, [[1, F]])
            vprev = sV.n
            aw = sA.take(1)

            def sc1(e, xf2=xf2, vprev=vprev):
                e.wait_ge(sV.h, vprev)
                e.activation(out=sgn, in_=xf2, func=AF.Sign).then_inc(sA.h, 1)
            p.on("scalar", sc1)
            sV.take(1)

            def v1(e, xf3=xf3, q3=q3, aw=aw, u=u, u3=u3, g=g, g3=g3, sgn=sgn, gguard=gguard,
                   am2=am2, am3=am3, sc=sc, sc_b=sc_b, rc=rc, rc_b=rc_b):
                e.drain()
                e.tensor_reduce(out=am3, in_=xf3, axis=AX.X, op=ALU.max,
                                apply_absolute_value=True)
                e.drain()
                e.tensor_scalar(out=sc, in0=am2, scalar1=1.0 / 6.0,
                                scalar2=1e-12, op0=ALU.mult, op1=ALU.max)
                e.drain()
                e.reciprocal(out=rc, in_=sc)
                e.drain()
                e.scalar_tensor_tensor(out=u, in0=xf2, scalar=-1.0, in1=xf2,
                                       op0=ALU.mult, op1=ALU.max)
                e.drain()
                e.tensor_tensor(out=u3, in0=u3, in1=rc_b, op=ALU.mult)
                e.drain()
                e.tensor_scalar(out=g, in0=u, scalar1=0.25, scalar2=None,
                                op0=ALU.is_gt)
                for t in (0.75, 1.25, 1.75):
                    e.drain()
                    e.scalar_tensor_tensor(out=g, in0=u, scalar=t, in1=g,
                                           op0=ALU.is_gt, op1=ALU.add)
                e.drain()
                e.tensor_scalar(out=g, in0=g, scalar1=0.5, scalar2=None,
                                op0=ALU.mult)
                for t in (2.5, 3.5, 5.0, 5.0):
                    e.drain()
                    e.scalar_tensor_tensor(out=g, in0=u, scalar=t,
                                           in1=g, op0=ALU.is_gt,
                                           op1=ALU.add)
                e.wait_ge(sA.h, aw)
                e.drain()
                e.tensor_tensor(out=g, in0=g, in1=sgn, op=ALU.mult)
                if gguard is not None:
                    e.wait_ge(sG.h, gguard)
                e.drain()
                e.tensor_tensor(out=q3, in0=g3, in1=sc_b,
                                op=ALU.mult).then_inc(sV.h, 1)
            p.on("vector", v1)
            return sV.n

        sG.take(80)

        def p0_gp(e):
            e.dma_start(
                out=bfv(A_WQ, [[DM, 8], [1, DM]]),
                in_=bass.AP(wq_d.tensor, 0,
                            [[DM, 128], [DM * 128, 8], [1, DM]])
            ).then_inc(sG.h, 16)
            e.dma_start(
                out=bfv(A_WK, [[DM, 8], [1, DM]]),
                in_=bass.AP(wkv_d.tensor, 0,
                            [[2 * DM, 128], [2 * DM * 128, 8], [1, DM]])
            ).then_inc(sG.h, 16)
            e.dma_start(
                out=bfv(A_WV, [[DM, 8], [1, DM]]),
                in_=bass.AP(wkv_d.tensor, DM,
                            [[2 * DM, 128], [2 * DM * 128, 8], [1, DM]])
            ).then_inc(sG.h, 16)
            e.dma_start(
                out=bfv(A_WO, [[DM, 8], [1, DM]]),
                in_=bass.AP(wo_d.tensor, 0,
                            [[DM, 128], [DM * 128, 8], [1, DM]])
            ).then_inc(sG.h, 16)
            e.dma_start(out=bo_sb[:], in_=bo_d).then_inc(sG.h, 16)
        p.on("gpsimd", p0_gp)
        sL.take(32)
        p.on("sync", lambda e: (e.dma_start(out=mask_sb[:],
                                            in_=mask_d).then_inc(sL.h, 16),
                                e.dma_start(out=ident_sb[:],
                                            in_=ident_d).then_inc(sL.h, 16)))
        p.on("vector", lambda e: (e.memset(ones_c[:], 1.0),
                                  e.memset(ones_r[:], 1.0),
                                  e.memset(cneg3[:], -3.0),
                                  e.memset(ceighth[:], 0.125),
                                  e.memset(bfv(A_VT + 64,
                                               [[1040, 5], [65, 16], [1, 1]]),
                                           1.0)))

        wjobs = ([(wg_d, wgqs, i, DM) for i in range(FI // 128)]
                 + [(wu_d, wuqs, i, DM) for i in range(FI // 128)]
                 + [(wd_d, wdqs, i, FI) for i in range(DM // 128)])
        jbuf = (X_SB, X_SB + 4096)
        jdone = []
        jstore = []
        for jidx, (src_d, dst_s, i, F) in enumerate(wjobs):
            nb = F // 16
            boff = jbuf[jidx % 2]
            sLx = sL if jidx % 2 == 0 else sL2
            guards = jdone[jidx - 2] if jidx >= 2 else (0, 0)
            lw = sLx.take(16)

            def ld(e, src_d=src_d, i=i, F=F, boff=boff, guards=guards,
                   sLx=sLx):
                e.wait_ge(sV.h, guards[0])
                e.wait_ge(sA.h, guards[1])
                e.dma_start(
                    out=fv(boff, [[1, F]]),
                    in_=bass.AP(src_d.tensor, i * 128 * F,
                                [[F, 128], [1, F]])).then_inc(sLx.h, 16)
            p.on("sync", ld)
            p.on("vector", lambda e, lw=lw, sLx=sLx: e.wait_ge(sLx.h, lw))
            p.on("scalar", lambda e, lw=lw, sLx=sLx: e.wait_ge(sLx.h, lw))
            qdone = emit_quant(fv(boff, [[1, F]]),
                               fv(boff, [[16, nb], [1, 16]]),
                               bfv(F_H, [[16, nb], [1, 16]]), F,
                               gguard=(jstore[-1] if jstore else None))
            jdone.append((sV.n, sA.n))
            jstore.append(sG.take(16))

            def st(e, dst_s=dst_s, i=i, F=F, vq=qdone):
                e.wait_ge(sV.h, vq)
                e.dma_start(
                    out=bass.AP(dst_s.tensor, i * 128 * F, [[F, 128], [1, F]]),
                    in_=bfv(F_H, [[1, F]])).then_inc(sG.h, 16)
            p.on("gpsimd", st)

        barrier()

        exp_bank_guard = [0, 0]
        pt_slot_guard = [0, 0]
        ce = {"tw": 0, "tw2": 0, "wo": 0, "vres": 0, "s2gq": 0,
              "carry": 0, "pdone": 0}
        ce_hist = []
        lbases = {}

        def emit_s1_load(cc):
            guards = ce_hist[cc - 2] if cc >= 2 else (0, 0, 0)
            lbase = sL.take(TPC * 16)
            lbases[cc] = lbase
            xoff = X_SB + (cc % 2) * 4096

            def s1_load(e, cc=cc, guards=guards, xoff=xoff, lbase=lbase):
                e.wait_ge(sL.h, lbase - TPC * 16)
                e.wait_ge(sV.h, guards[0])
                e.wait_ge(sA.h, guards[1])
                e.wait_ge(sG.h, guards[2])
                for t in range(TPC):
                    e.dma_start(
                        out=fv(xoff + t * DM, [[1, DM]]),
                        in_=x_d[cc * CHUNK + t * 128:
                                cc * CHUNK + (t + 1) * 128, :]
                    ).then_inc(sL.h, 16)
            p.on("sync", s1_load)

        def emit_ln1(cc):
            xoff = X_SB + (cc % 2) * 4096
            lbase = lbases[cc]
            for t in range(TPC):
                gprev = sG.n
                p.on("vector", lambda e, lbase=lbase, gprev=gprev:
                     (e.wait_ge(sL.h, lbase), e.wait_ge(sG.h, gprev)))
                p.on("scalar", lambda e, lbase=lbase: e.wait_ge(sL.h, lbase))
                emit_ln(fv(xoff + t * DM, [[1, DM]]),
                        bfv(A_XN1 + (t % 2) * DM, [[1, DM]]),
                        bfv(A_JNK, [[1, DM]]))
                vq = sV.n
                sG.take(16)

                def st(e, t=t, vq=vq, tg=ce["tw"]):
                    e.wait_ge(sV.h, vq)
                    e.wait_ge(sT.h, tg)
                    e.dma_start(
                        out=xn1s[t * 128:(t + 1) * 128, :],
                        in_=bfv(A_XN1 + (t % 2) * DM, [[1, DM]])
                    ).then_inc(sG.h, 16)
                p.on("gpsimd", st)
            ce["s2gq"] = sG.n

        def emit_s2(cc):
            tpre1 = sT.n
            tw = sT.take(8 * 16)
            ce["tw"] = tw

            def s2(e, gq=ce["s2gq"], wg=ce["wo"], tpre1=tpre1):
                e.wait_ge(sG.h, gq)
                e.wait_ge(sP.h, wg)
                e.wait_ge(sT.h, tpre1)
                for k in range(8):
                    e.dma_start_transpose(
                        out=bfv(A_XNT + k * CHUNK, [[1, CHUNK]]),
                        in_=xn1s[:, k * 128:(k + 1) * 128]).then_inc(sT.h, 16)
            p.on("sync", s2)
            return tw

        def emit_carry(cc):
            new_seq = (cc % cps == 0)
            carry_done = sV.take(1)
            ce["carry"] = carry_done

            def s3_carry(e, new_seq=new_seq, pdone=ce["pdone"]):
                e.wait_ge(sA.h, pdone)
                if new_seq:
                    e.memset(bfv(A_KT, [[640, 8], [1, 128]]), 0.0)
                    e.drain()
                    e.memset(bfv(A_VT, [[65, 16], [1, 64]]), 0.0)
                else:
                    e.tensor_copy(out=bfv(A_KT, [[640, 8], [1, 128]]),
                                  in_=bfv(A_KT + 4 * 128, [[640, 8], [1, 128]]))
                    e.drain()
                    e.tensor_copy(out=bfv(A_VT, [[65, 16], [1, 64]]),
                                  in_=bfv(A_VT + 4 * 1040, [[65, 16], [1, 64]]))
                e.engine_nop().then_inc(sV.h, 1)
            p.on("vector", s3_carry)

        def make_projs(cc, tw):
            jobs = []
            for woff, base, pitch in ((A_WQ, A_QT, 512), (A_WK, A_KT, 640)):
                for m in range(8):
                    dsto = base + m * pitch + (128 if base == A_KT else 0)
                    jobs.append((
                        (lambda k, woff=woff, m=m:
                         bfv(woff + k * DM + m * 128, [[1, 128]])),
                        (lambda k: bfv(A_XNT + k * CHUNK, [[1, CHUNK]])),
                        dsto))
            for t in range(TPC):
                for half in range(2):
                    jobs.append((
                        (lambda k, t=t: bfv(A_XNT + k * CHUNK + t * 128,
                                            [[1, 128]])),
                        (lambda k, half=half: bfv(A_WV + k * DM + half * 512,
                                                  [[1, 512]])),
                        (A_VT + (1 + t) * 1040 + half * 520, "v")))
            state = {"slot": 0, "first": True}
            evac_hist = []

            def emit_group(lo, hi, tw=tw):
                for j in range(lo, hi):
                    lhs_fn, rhs_fn, dst_off = jobs[j]
                    col = PS_MM0 if state["slot"] == 0 else PS_MM1
                    bank_guard = (evac_hist[-2] if len(evac_hist) >= 2
                                  else None)
                    pw = sP.take(1)
                    first = state["first"]
                    state["first"] = False

                    def mm(e, col=col, bank_guard=bank_guard, first=first,
                           lhs_fn=lhs_fn, rhs_fn=rhs_fn, tw=tw,
                           carry_done=ce["carry"], vres_g=ce["vres"]):
                        if first:
                            e.wait_ge(sT.h, tw)
                            e.wait_ge(sV.h, carry_done)
                            e.wait_ge(sV.h, vres_g)
                        if bank_guard is not None:
                            e.wait_ge(sA.h, bank_guard)
                        for k in range(8):
                            last = e.matmul(ps[:, col:col + 512], lhs_fn(k),
                                            rhs_fn(k), start=(k == 0),
                                            stop=(k == 7))
                        last.then_inc(sP.h, 1)
                    p.on("tensor", mm)
                    ev_n = sA.take(1)

                    def ev(e, col=col, dst_off=dst_off, pw=pw):
                        e.wait_ge(sP.h, pw)
                        if isinstance(dst_off, tuple):
                            dst = bfv(dst_off[0], [[65, 8], [1, 64]])
                            src = bass.AP(ps, col,
                                          [[4096, 128], [64, 8], [1, 64]])
                        else:
                            dst = bfv(dst_off, [[1, 512]])
                            src = ps[:, col:col + 512]
                        e.activation(out=dst, in_=src,
                                     func=AF.Copy).then_inc(sA.h, 1)
                    p.on("scalar", ev)
                    evac_hist.append(ev_n)
                    state["slot"] ^= 1
                if hi == len(jobs):
                    ce["pdone"] = sA.n
            return emit_group

        def emit_windows(cc):
            for w in range(WPC):
                norm_prev = sV.n
                sim_incs = {}

                def emit_sim(pp, w=w, norm_prev=norm_prev,
                             proj_done=ce["pdone"]):
                    pw = sP.take(1)
                    sim_incs[pp] = pw
                    guard = exp_bank_guard[pp % 2]

                    def mm_sim(e, p=pp, w=w, guard=guard,
                               norm_prev=norm_prev, proj_done=proj_done):
                        if p == 0:
                            e.wait_ge(sA.h, proj_done)
                        if guard:
                            e.wait_ge(sA.h, guard)
                        simc = PS_SIMA + (p % 2) * 1024
                        for j2 in range(2):
                            sub = j2 * 64
                            for j, sl in enumerate((w, w + 1)):
                                e.matmul(
                                    ps[:, simc + j2 * 512 + j * 128:
                                       simc + j2 * 512 + j * 128 + 128],
                                    bass.AP(bfar, sub * NBF + A_KT + p * 640
                                            + sl * 128,
                                            [[NBF, 64], [1, 128]]),
                                    bass.AP(bfar, sub * NBF + A_QT + p * 512
                                            + w * 128, [[NBF, 64], [1, 128]]),
                                    start=True, stop=(j == 0))
                            last = e.matmul(
                                ps[:, simc + j2 * 512 + 128:
                                   simc + j2 * 512 + 256],
                                ident_sb[:], mask_sb[:],
                                start=False, stop=True)
                        last.then_inc(sP.h, 1)
                    p.on("tensor", mm_sim)

                emit_sim(0)
                emit_sim(1)
                for p0 in range(8):
                    simc = PS_SIMA + (p0 % 2) * 1024
                    pt_off = A_PT + (p0 % 2) * 512
                    aw = sA.take(1)
                    exp_bank_guard[p0 % 2] = aw
                    slot_guard = pt_slot_guard[p0 % 2]

                    def act_exp(e, pt_off=pt_off, simc=simc,
                                pw=sim_incs[p0], slot_guard=slot_guard):
                        e.wait_ge(sP.h, pw)
                        if slot_guard:
                            e.wait_ge(sP.h, slot_guard)
                        e.activation(out=bfv(pt_off, [[1, 512]]),
                                     in_=bass.AP(ps, simc,
                                                 [[4096, 128], [512, 2],
                                                  [1, 256]]),
                                     func=AF.Exp, bias=cneg3[:],
                                     scale=ceighth[:]).then_inc(sA.h, 1)
                    p.on("scalar", act_exp)
                    pw2 = sP.take(1)
                    pt_slot_guard[p0 % 2] = pw2

                    def mm_av(e, pt_off=pt_off, w=w, p=p0, aw=aw,
                              norm_prev=norm_prev):
                        e.wait_ge(sA.h, aw)
                        if p == 0:
                            e.wait_ge(sV.h, norm_prev)
                        for j2 in range(2):
                            h = 2 * p + j2
                            for j, sl in enumerate((w, w + 1)):
                                lhsT = bfv(pt_off + j2 * 256 + j * 128,
                                           [[1, 128]])
                                last = e.matmul(
                                    ps[:, PS_AVS + h * 128:
                                       PS_AVS + h * 128 + 65],
                                    lhsT,
                                    bfv(A_VT + sl * 1040 + h * 65, [[1, 65]]),
                                    start=(j == 0), stop=(j == 1))
                        last.then_inc(sP.h, 1)
                    p.on("tensor", mm_av)
                    if p0 + 2 < 8:
                        emit_sim(p0 + 2)
                pw = sP.n
                gprev = sG.n
                vq = sV.take(1)

                def v_norm(e, w=w, pw=pw, gprev=gprev):
                    e.wait_ge(sP.h, pw)
                    e.wait_ge(sG.h, gprev)
                    e.reciprocal(out=fv(RS, [[1, H]]),
                                 in_=bass.AP(ps, 64, [[4096, 128], [128, 16],
                                                      [1, 1]]))
                    e.drain()
                    e.scalar_tensor_tensor(
                        out=bfv(A_XN1 + (w % 2) * DM, [[1, DM]]),
                        in0=bass.AP(ps, 0, [[4096, 128], [128, 16], [1, 64]]),
                        scalar=1.0,
                        in1=fv(RS, [[1, H], [0, 64]]),
                        op0=ALU.mult, op1=ALU.mult).then_inc(sV.h, 1)
                p.on("vector", v_norm)
                sG.take(16)

                def g_st(e, w=w, vq=vq, tg=ce["tw2"]):
                    e.wait_ge(sV.h, vq)
                    e.wait_ge(sT.h, tg)
                    e.dma_start(
                        out=aos[w * 128:(w + 1) * 128, :],
                        in_=bfv(A_XN1 + (w % 2) * DM, [[1, DM]])
                    ).then_inc(sG.h, 16)
                p.on("gpsimd", g_st)

            gq = sG.n
            tpre2 = sT.n
            tw2 = sT.take(8 * 16)
            ce["tw2"] = tw2

            def s5(e, gq=gq, tpre2=tpre2):
                e.wait_ge(sG.h, gq)
                e.wait_ge(sT.h, tpre2)
                for k in range(8):
                    e.dma_start_transpose(
                        out=bfv(A_XNT + k * CHUNK, [[1, CHUNK]]),
                        in_=aos[:, k * 128:(k + 1) * 128]).then_inc(sT.h, 16)
            p.on("sync", s5)
            return tw2

        def emit_tail(cc, tw2):
            xoff = X_SB + (cc % 2) * 4096
            has_next = (cc + 1 < n_chunks)
            wo_pws = []
            vres_vqs = []
            for t in range(TPC):
                vprev = vres_vqs[t - 1] if t > 0 else 0
                pw = sP.take(1)
                wo_pws.append(pw)

                def mm_wo(e, t=t, vprev=vprev, tw2=tw2):
                    e.wait_ge(sT.h, tw2)
                    if t > 0:
                        e.wait_ge(sV.h, vprev)
                    for half in range(2):
                        for k in range(8):
                            e.matmul(
                                ps[:, PS_WO + half * 512:
                                   PS_WO + half * 512 + 512],
                                bfv(A_XNT + k * CHUNK + t * 128, [[1, 128]]),
                                bfv(A_WO + k * DM + half * 512, [[1, 512]]),
                                start=(k == 0), stop=False)
                        last = e.matmul(ps[:, PS_WO + half * 512:
                                           PS_WO + half * 512 + 512],
                                        ones_r[:],
                                        bo_sb[:, half * 512:(half + 1) * 512],
                                        start=False, stop=True)
                    last.then_inc(sP.h, 1)
                p.on("tensor", mm_wo)
                vq = sV.take(1)
                vres_vqs.append(vq)

                def v_res(e, t=t, pw=pw, xoff=xoff):
                    e.wait_ge(sP.h, pw)
                    e.tensor_tensor(out=fv(xoff + t * DM, [[1, DM]]),
                                    in0=fv(xoff + t * DM, [[1, DM]]),
                                    in1=ps[:, PS_WO:PS_WO + DM],
                                    op=ALU.add).then_inc(sV.h, 1)
                p.on("vector", v_res)
                sG.take(16)

                def g_st(e, cc=cc, t=t, vq=vq, xoff=xoff):
                    e.wait_ge(sV.h, vq)
                    e.dma_start(
                        out=x1s[cc * CHUNK + t * 128:
                                cc * CHUNK + (t + 1) * 128, :],
                        in_=fv(xoff + t * DM, [[1, DM]])).then_inc(sG.h, 16)
                p.on("gpsimd", g_st)
            ce["wo"] = sP.n
            ce["vres"] = vres_vqs[-1]
            emit_pg = None
            if has_next:
                emit_carry(cc + 1)
                emit_ln1(cc + 1)
            for t in range(TPC):
                emit_ln(fv(xoff + t * DM, [[1, DM]]), fv(OUTB, [[1, DM]]),
                        bfv(A_JNK, [[1, DM]]))
                if t == 0 and has_next:
                    tw_n = emit_s2(cc + 1)
                    emit_pg = make_projs(cc + 1, tw_n)
                qdone = emit_quant(fv(OUTB, [[1, DM]]),
                                   fv(OUTB, [[16, 64], [1, 16]]),
                                   bfv(F_XQ, [[16, 64], [1, 16]]), DM,
                                   uoff=PS_AVS, gguard=sG.n)
                sG.take(16)

                def g_st2(e, cc=cc, t=t, vq2=qdone):
                    e.wait_ge(sV.h, vq2)
                    e.dma_start(
                        out=xqs[cc * CHUNK + t * 128:
                                cc * CHUNK + (t + 1) * 128, :],
                        in_=bfv(F_XQ, [[1, DM]])).then_inc(sG.h, 16)
                p.on("gpsimd", g_st2)
                if emit_pg is not None:
                    emit_pg(t * 6, (t + 1) * 6)
            ce_hist.append((sV.n, sA.n, sG.n))

        emit_s1_load(0)
        for c in range(n_chunks):
            if c + 1 < n_chunks:
                emit_s1_load(c + 1)
            if c == 0:
                emit_carry(0)
                emit_ln1(0)
                tw0 = emit_s2(0)
                pg0 = make_projs(0, tw0)
                pg0(0, 24)
            tw2 = emit_windows(c)
            emit_tail(c, tw2)

        barrier()
        sT.take(38 * 16)

        def p2_w(e):
            for k in range(8):
                e.dma_start_transpose(
                    out=bfv(WGQ + k * FI, [[1, FI]]),
                    in_=wgqs[:, k * 128:(k + 1) * 128]).then_inc(sT.h, 16)
            for k in range(8):
                e.dma_start_transpose(
                    out=bfv(WUQ + k * FI, [[1, FI]]),
                    in_=wuqs[:, k * 128:(k + 1) * 128]).then_inc(sT.h, 16)
            for cc in range(22):
                e.dma_start_transpose(
                    out=bfv(WDQ + cc * DM, [[1, DM]]),
                    in_=wdqs[:, cc * 128:(cc + 1) * 128]).then_inc(sT.h, 16)
        p.on("scalar", p2_w)
        barrier()

        nsl = [(i * 512, min(512, FI - i * 512))
               for i in range((FI + 511) // 512)]
        vres2_last = [sV.n]
        up_last = [0]
        lhqt_last = [(sT.n, sT2.n)]
        for c in range(n_chunks):
            lbase = sL.take(TPC * 16)

            def l_x1(e, c=c, vguard=vres2_last[0]):
                e.wait_ge(sV.h, vguard)
                for t in range(TPC):
                    e.dma_start(
                        out=fv(X_SB + t * DM, [[1, DM]]),
                        in_=x1s[c * CHUNK + t * 128:
                                c * CHUNK + (t + 1) * 128, :]
                    ).then_inc(sL.h, 16)
            p.on("sync", l_x1)

            tpre = sT.n
            tw = sT.take(8 * 16)

            def l_xqt(e, c=c, pguard=up_last[0], tpre=tpre):
                e.wait_ge(sP.h, pguard)
                e.wait_ge(sT.h, tpre)
                for k in range(8):
                    e.dma_start_transpose(
                        out=bfv(F_XQT + k * CHUNK, [[1, CHUNK]]),
                        in_=xqs[c * CHUNK:(c + 1) * CHUNK,
                                k * 128:(k + 1) * 128]).then_inc(sT.h, 16)
            p.on("scalar", l_xqt)

            vh_inc = [vres2_last[0]]
            for t in range(TPC):
                vprev = vh_inc[0]
                pw = sP.take(1)

                def mm_gate(e, t=t, vprev=vprev, tw=tw):
                    e.wait_ge(sT.h, tw)
                    e.wait_ge(sV.h, vprev)
                    for ns, nw in nsl:
                        for k in range(8):
                            last = e.matmul(
                                ps[:, ns:ns + nw],
                                bfv(F_XQT + k * CHUNK + t * 128, [[1, 128]]),
                                bfv(WGQ + k * FI + ns, [[1, nw]]),
                                start=(k == 0), stop=(k == 7))
                    last.then_inc(sP.h, 1)
                p.on("tensor", mm_gate)
                aw = sA.take(1)

                def a_silu(e, pw=pw):
                    e.wait_ge(sP.h, pw)
                    for ns, nw in nsl:
                        last = e.activation(out=bfv(F_SG + ns, [[1, nw]]),
                                            in_=ps[:, ns:ns + nw],
                                            func=AF.Silu)
                    last.then_inc(sA.h, 1)
                p.on("scalar", a_silu)
                pw2 = sP.take(1)

                def mm_up(e, t=t, aw=aw):
                    e.wait_ge(sA.h, aw)
                    for ns, nw in nsl:
                        for k in range(8):
                            last = e.matmul(
                                ps[:, ns:ns + nw],
                                bfv(F_XQT + k * CHUNK + t * 128, [[1, 128]]),
                                bfv(WUQ + k * FI + ns, [[1, nw]]),
                                start=(k == 0), stop=(k == 7))
                    last.then_inc(sP.h, 1)
                p.on("tensor", mm_up)
                up_last[0] = pw2
                gprev = sG.n
                vh_inc[0] = sV.take(1)

                def v_h(e, pw2=pw2, gprev=gprev):
                    e.wait_ge(sP.h, pw2)
                    e.wait_ge(sG.h, gprev)
                    e.tensor_tensor(out=bfv(F_H, [[1, FI]]),
                                    in0=ps[:, 0:FI],
                                    in1=bfv(F_SG, [[1, FI]]),
                                    op=ALU.mult).then_inc(sV.h, 1)
                p.on("vector", v_h)
                for qoff, Fs in ((0, 1024), (1024, 1024), (2048, 768)):
                    qdone = emit_quant(
                        bfv(F_H + qoff, [[1, Fs]]),
                        bfv(F_H + qoff, [[16, Fs // 16], [1, 16]]),
                        bfv(F_H + qoff, [[16, Fs // 16], [1, 16]]),
                        Fs, usb=X_SB + 4096 + 2048, nboff=qoff // 16)
                sG.take(16)

                def st(e, t=t, vq=qdone, tguard=lhqt_last[0]):
                    e.wait_ge(sV.h, vq)
                    e.wait_ge(sT.h, tguard[0])
                    e.wait_ge(sT2.h, tguard[1])
                    e.dma_start(
                        out=hqs[t * 128:(t + 1) * 128, :],
                        in_=bfv(F_H, [[1, FI]])).then_inc(sG.h, 16)
                p.on("gpsimd", st)

            hq_done = sG.n
            vlast = vh_inc[0]
            dn_pe = []
            for cc in range(22):
                sl = cc % 4
                guard = dn_pe[cc - 4] if cc >= 4 else None
                sTq = sT if cc % 2 == 0 else sT2
                qeng = "scalar" if cc % 2 == 0 else "sync"
                tw3 = sTq.take(16)

                def l_hqt(e, cc=cc, sl=sl, guard=guard, hq_done=hq_done,
                          tw3=tw3, sTq=sTq):
                    if cc <= 1:
                        e.wait_ge(sG.h, hq_done)
                    if guard is not None:
                        e.wait_ge(sP.h, guard)
                    e.wait_ge(sTq.h, tw3 - 16)
                    e.dma_start_transpose(
                        out=bfv(F_HQT + sl * 512, [[1, 512]]),
                        in_=hqs[:, cc * 128:(cc + 1) * 128]).then_inc(sTq.h,
                                                                      16)
                p.on(qeng, l_hqt)
                pq = sP.take(1)

                def mm_dn(e, cc=cc, sl=sl, tw3=tw3, vlast=vlast, sTq=sTq):
                    if cc == 0:
                        e.wait_ge(sV.h, vlast)
                    e.wait_ge(sTq.h, tw3)
                    for t in range(TPC):
                        for half in range(2):
                            last = e.matmul(
                                ps[:, PS_DN + t * DM + half * 512:
                                   PS_DN + t * DM + half * 512 + 512],
                                bfv(F_HQT + sl * 512 + t * 128, [[1, 128]]),
                                bfv(WDQ + cc * DM + half * 512, [[1, 512]]),
                                start=(cc == 0), stop=(cc == 21))
                    last.then_inc(sP.h, 1)
                p.on("tensor", mm_dn)
                dn_pe.append(pq)
            lhqt_last[0] = (sT.n, sT2.n)
            pw = sP.n

            for t in range(TPC):
                gprev2 = sG.n
                vq = sV.take(1)

                def v_res(e, t=t, pw=pw, gprev2=gprev2, lbase=lbase):
                    e.wait_ge(sP.h, pw)
                    e.wait_ge(sG.h, gprev2)
                    e.wait_ge(sL.h, lbase)
                    e.tensor_tensor(
                        out=fv(X_SB + 4096 + (t % 2) * DM, [[1, DM]]),
                        in0=fv(X_SB + t * DM, [[1, DM]]),
                        in1=ps[:, PS_DN + t * DM:PS_DN + (t + 1) * DM],
                        op=ALU.add).then_inc(sV.h, 1)
                p.on("vector", v_res)
                sG.take(16)

                def g_st(e, c=c, t=t, vq=vq):
                    e.wait_ge(sV.h, vq)
                    e.dma_start(
                        out=out_d[c * CHUNK + t * 128:
                                  c * CHUNK + (t + 1) * 128, :],
                        in_=fv(X_SB + 4096 + (t % 2) * DM, [[1, DM]])
                    ).then_inc(sG.h, 16)
                p.on("gpsimd", g_st)
            vres2_last[0] = sV.n

        gq = sG.n
        p.on("gpsimd", lambda e, gq=gq: e.wait_ge(sG.h, gq))
        p.emit()
    return nc


def _mask01():
    jc = np.arange(W)[:, None]
    i = np.arange(W)[None, :]
    return np.where(i >= jc, 0.0, -30000.0).astype(ml_dtypes.bfloat16)


def _ident():
    return np.eye(W, dtype=ml_dtypes.bfloat16)


def kernel(L0_attn_x, L0_attn_wq, L0_attn_wkv, L0_attn_wo, L0_attn_bo,
           L0_ffn_w_gate, L0_ffn_w_up, L0_ffn_w_down):
    x = np.ascontiguousarray(np.asarray(L0_attn_x, dtype=np.float32))
    B, N, D = x.shape
    n_seq = B // NCORES
    shards = x.reshape(NCORES, n_seq * N, D)
    wq = np.ascontiguousarray(np.asarray(L0_attn_wq, dtype=np.float32))
    wkv = np.ascontiguousarray(np.asarray(L0_attn_wkv, dtype=np.float32))
    wo = np.ascontiguousarray(np.asarray(L0_attn_wo, dtype=np.float32))
    bo = np.ascontiguousarray(
        np.asarray(L0_attn_bo, dtype=np.float32)).reshape(1, DM)
    wg = np.ascontiguousarray(np.asarray(L0_ffn_w_gate, dtype=np.float32))
    wu = np.ascontiguousarray(np.asarray(L0_ffn_w_up, dtype=np.float32))
    wd = np.ascontiguousarray(np.asarray(L0_ffn_w_down, dtype=np.float32))
    m01 = _mask01()
    idn = _ident()

    nc = build_program(n_seq)
    in_maps = [{"x": shards[i], "wq": wq, "wkv": wkv, "wo": wo, "bo": bo,
                "wgate": wg, "wup": wu, "wdown": wd, "mask01": m01,
                "ident": idn}
               for i in range(NCORES)]
    res = run_bass_kernel_spmd(nc, in_maps, core_ids=list(range(NCORES)))
    out = np.concatenate([np.asarray(r["out"]) for r in res.results], axis=0)
    return out.reshape(B, N, D).astype(np.float32)
